# version 13
# baseline (speedup 1.0000x reference)
import numpy as np
from contextlib import ExitStack

import concourse.bass as bass
import concourse.mybir as mybir
import concourse.bass_utils as _bass_utils
from concourse.bass_utils import run_bass_kernel_spmd

if not getattr(_bass_utils, "_ldwopt_patched", False):
    _orig_run_command = _bass_utils.run_command

    def _run_command_ldwopt(argv, **kwargs):
        argv = ["--enable-ldw-opt=true" if a == "--enable-ldw-opt=false" else a
                for a in argv]
        return _orig_run_command(argv, **kwargs)

    _bass_utils.run_command = _run_command_ldwopt
    _bass_utils._ldwopt_patched = True

F32 = mybir.dt.float32
BF16 = mybir.dt.bfloat16
I32 = mybir.dt.int32
AF = mybir.ActivationFunctionType
ALU = mybir.AluOpType
AX = mybir.AxisListType

D = 1024
DH = 2048
NE = 8
FF = 1408
T = 512
NSUB = 4
C8 = 8
C16 = 16
C11 = 11
TWO_PI = float(2.0 * np.pi)

ENGS = ("sync", "gpsimd", "scalar", "vector", "tensor")


class Sem:
    def __init__(self, h, name):
        self.h = h
        self.n = 0
        self.name = name


class Buf:
    __slots__ = ("name", "prod", "prod_eng", "readers")

    def __init__(self, name):
        self.name = name
        self.prod = None
        self.prod_eng = None
        self.readers = {}


class Emitter:
    def __init__(self, prog_sems):
        self.cmds = {e: [] for e in ENGS}
        self.waited = {e: {} for e in ENGS}
        self.prog = prog_sems
        self.pend_r = {e: set() for e in ENGS}
        self.pend_w = {e: set() for e in ENGS}

    def raw(self, eng, fn, wait=(), inc=()):
        w = []
        for s, v in wait:
            if v <= 0 or self.waited[eng].get(id(s), -1) >= v:
                continue
            self.waited[eng][id(s)] = v
            w.append((s, v))
        for s, a in inc:
            s.n += a

        def cmd(h, fn=fn, w=tuple(w), inc=tuple(inc)):
            for s, v in w:
                h.wait_ge(s.h, v)
            ins = fn(h)
            for s, a in inc:
                ins = ins.then_inc(s.h, a)
        self.cmds[eng].append(cmd)

    def drain(self, eng):
        self.cmds[eng].append(lambda h: h.drain())

    def barrier(self, eng):
        self.drain(eng)
        sem = self.prog[eng]
        sem.n += 1
        self.cmds[eng].append(lambda h, sem=sem: h.nop().then_inc(sem.h, 1))
        self.cover(eng)

    def op(self, eng, fn, reads=(), writes=(), wait=(), inc=(), ms=False,
           dma_sem=None):
        waits = list(wait)
        for b in reads:
            for e2 in ENGS:
                if e2 != eng:
                    assert b not in self.pend_w[e2], \
                        f"{eng} reads {b.name} pending-written by {e2}"
            if b.prod is not None and b.prod_eng != eng:
                waits.append(b.prod)
        for b in writes:
            for e2 in ENGS:
                if e2 != eng:
                    assert b not in self.pend_w[e2], \
                        f"{eng} writes {b.name} pending-written by {e2}"
                    assert b not in self.pend_r[e2], \
                        f"{eng} writes {b.name} pending-read by {e2}"
            for k, sv in b.readers.items():
                if k != eng:
                    waits.append(sv)
            if b.prod is not None and b.prod_eng != eng:
                waits.append(b.prod)

        incs = list(inc)
        soft_ms = ms and eng in ("vector", "scalar", "gpsimd")
        if ms and not soft_ms:
            incs.append((self.prog[eng], 1))
        if dma_sem is not None:
            incs.append((dma_sem, 16))
        self.raw(eng, fn, wait=waits, inc=incs)

        if dma_sem is not None:
            key = "dma:" + dma_sem.name
            for b in writes:
                b.prod = (dma_sem, dma_sem.n)
                b.prod_eng = key
                b.readers = {}
            for b in reads:
                b.readers[key] = (dma_sem, dma_sem.n)
            return
        for b in reads:
            self.pend_r[eng].add(b)
        for b in writes:
            self.pend_w[eng].add(b)
        if ms:
            if soft_ms:
                self.barrier(eng)
            else:
                self.cover(eng)

    def cover(self, eng):
        s = self.prog[eng]
        for b in self.pend_r[eng]:
            b.readers[eng] = (s, s.n)
        for b in self.pend_w[eng]:
            b.prod = (s, s.n)
            b.prod_eng = eng
            b.readers = {k: v for k, v in b.readers.items() if k == eng}
            b.readers[eng] = (s, s.n)
        self.pend_r[eng].clear()
        self.pend_w[eng].clear()


def build_program(n_seq=2, n_tile=8, debug=False):
    S = n_tile * T
    nc = bass.Bass("TRN2", target_bir_lowering=False)

    x_d = nc.dram_tensor("x", [n_seq, S, D], F32, kind="ExternalInput").ap()
    w_in_d = nc.dram_tensor("w_in", [D, DH], F32, kind="ExternalInput").ap()
    b_in_d = nc.dram_tensor("b_in", [DH], F32, kind="ExternalInput").ap()
    w_gx_d = nc.dram_tensor("w_gx", [NE, 256, 128], F32, kind="ExternalInput").ap()
    b_gx_d = nc.dram_tensor("b_gx", [D], F32, kind="ExternalInput").ap()
    w_ga_d = nc.dram_tensor("w_ga", [NE, 256, 128], F32, kind="ExternalInput").ap()
    b_ga_d = nc.dram_tensor("b_ga", [D], F32, kind="ExternalInput").ap()
    arp_d = nc.dram_tensor("arp", [D], F32, kind="ExternalInput").ap()
    aip_d = nc.dram_tensor("aip", [D], F32, kind="ExternalInput").ap()
    w_out_d = nc.dram_tensor("w_out", [DH, D], F32, kind="ExternalInput").ap()
    b_out_d = nc.dram_tensor("b_out", [D], F32, kind="ExternalInput").ap()
    rw_d = nc.dram_tensor("rw", [D, NE], F32, kind="ExternalInput").ap()
    w1_d = nc.dram_tensor("w1", [NE, D, 2 * FF], F32, kind="ExternalInput").ap()
    w2_d = nc.dram_tensor("w2", [NE, FF, D], F32, kind="ExternalInput").ap()
    out_d = nc.dram_tensor("out", [n_seq, S, D], F32, kind="ExternalOutput").ap()
    if debug:
        dbg_xh = nc.dram_tensor("dbg_xh", [128, C16, T], BF16, kind="ExternalOutput").ap()
        dbg_gx = nc.dram_tensor("dbg_gx", [128, C8, T], F32, kind="ExternalOutput").ap()
        dbg_ga = nc.dram_tensor("dbg_ga", [128, C8, T], F32, kind="ExternalOutput").ap()
        dbg_hm = nc.dram_tensor("dbg_hm", [128, C16, T], BF16, kind="ExternalOutput").ap()
        dbg_xn = nc.dram_tensor("dbg_xn", [128, C8, T], BF16, kind="ExternalOutput").ap()
        dbg_ac = nc.dram_tensor("dbg_ac", [128, NSUB, D], F32, kind="ExternalOutput").ap()
        dbg_wo = nc.dram_tensor("dbg_wo", [DH, D], BF16, kind="ExternalOutput").ap()
        dbg_ps = nc.dram_tensor("dbg_ps", [128, T], F32, kind="ExternalOutput").ap()
        dbg_ac2 = nc.dram_tensor("dbg_ac2", [128, NSUB, D], F32, kind="ExternalOutput").ap()
        dbg_hm2 = nc.dram_tensor("dbg_hm2", [128, C16, T], BF16, kind="ExternalOutput").ap()
        dbg_wi = nc.dram_tensor("dbg_wi", [C8, 128, C8, 256], BF16, kind="ExternalOutput").ap()
        dbg_lp = nc.dram_tensor("dbg_lp", [128, 32], F32, kind="ExternalOutput").ap()

    w_ins_d = nc.dram_tensor("w_ins", [C8, 128, C8, 256], BF16, kind="Internal").ap()
    w_outs_d = nc.dram_tensor("w_outs", [DH, D], BF16, kind="Internal").ap()
    w1s_d = nc.dram_tensor("w1s", [NE, C11, 128, C8, 256], BF16, kind="Internal").ap()
    w2s_d = nc.dram_tensor("w2s", [NE, FF, D], BF16, kind="Internal").ap()

    with ExitStack() as stack:
        def sbt(name, shape, dt=F32):
            return stack.enter_context(nc.sbuf_tensor(name, shape, dt))

        def pst(name, shape, dt=F32):
            return stack.enter_context(nc.psum_tensor(name, shape, dt))

        ident = sbt("ident", [128, 128], BF16)
        identf = sbt("identf", [16, 16], F32)
        onesb = sbt("onesb", [1, T], BF16)
        cvec = sbt("cvec", [128, 6])
        b_in_sb = sbt("b_in_sb", [128, C16])
        b_gx_sb = sbt("b_gx_sb", [128, C8])
        b_ga_sb = sbt("b_ga_sb", [128, C8])
        sp8_sb = sbt("sp8_sb", [128, C8])
        ait_sb = sbt("ait_sb", [128, C8])
        vtmp_sb = sbt("vtmp_sb", [128, C8])
        b_out_row = sbt("b_out_row", [1, D], BF16)
        rw_sb = sbt("rw_sb", [128, C8, NE], BF16)
        wgx_sb = sbt("wgx_sb", [128, C8, 2, 128], BF16)
        wga_sb = sbt("wga_sb", [128, C8, 2, 128], BF16)
        vstage = sbt("vstage", [16, 128 * 5])
        carr_re = sbt("carr_re", [128, C8, n_seq])
        carr_im = sbt("carr_im", [128, C8, n_seq])
        idxc = sbt("idxc", [128, 32])
        idxmB = sbt("idxmB", [128, 32])
        idxi = sbt("idxi", [128, 32], I32)

        WIN_SLOTS, W1_SLOTS, W2_SLOTS, WOUT_SLOTS = 2, 4, 3, 4
        win_r = sbt("win_r", [128, WIN_SLOTS, C8, 256], BF16)
        w1_r = sbt("w1_r", [128, W1_SLOTS, C8, 256], BF16)
        w2_r = sbt("w2_r", [128, W2_SLOTS, D], BF16)
        wout_r = sbt("wout_r", [128, WOUT_SLOTS, D], BF16)

        acc = sbt("acc", [128, 2, NSUB, D])
        xn_tok = sbt("xn_tok", [128, NSUB, D], BF16)
        xnf = sbt("xnf", [128, C8, T], BF16)
        xhbf = sbt("xhbf", [128, C16, T], BF16)
        hm = sbt("hm", [128, C16, T], BF16)
        gx_all = sbt("gx_all", [128, C8, T], BF16)
        ga_all = sbt("ga_all", [128, C8, T])
        g_sb = sbt("g_sb", [128, C11, T], BF16)
        s1t = sbt("s1t", [128, 2, T], BF16)
        scr = sbt("scr", [128, D], BF16)
        stats2 = sbt("stats2", [128, 32])
        lg = sbt("lg", [128, 32])
        rex = sbt("rex", [128, 32])
        rpr = sbt("rpr", [128, 32])
        rt1 = sbt("rt1", [128, 32])
        rt2 = sbt("rt2", [128, 32])
        rt3 = sbt("rt3", [128, 32])
        lp_sb = sbt("lp_sb", [128, 32])
        rm4 = sbt("rm4", [128, 4])
        rs4 = sbt("rs4", [128, 4])
        rv4 = sbt("rv4", [128, 4])
        bL = sbt("bL", [128, T])
        bF = sbt("bF", [128, T])
        bM = sbt("bM", [128, T])
        bS = sbt("bS", [128, T])
        bC = sbt("bC", [128, T])
        bU1 = sbt("bU1", [128, T])
        bU2 = sbt("bU2", [128, T])
        bWre = sbt("bWre", [128, T])
        bWim = sbt("bWim", [128, T])
        bKGre = sbt("bKGre", [128, T])
        bIGim = sbt("bIGim", [128, T])
        bK = bKGre
        bGre = bKGre
        bGim = bIGim
        bI = bIGim[:].bitcast(I32)
        ctmp = sbt("ctmp", [128, 4])

        pBig = [pst(f"pBig{i}", [128, D]) for i in range(4)]

        def SBv(i):
            return pBig[i // 2][:, (i % 2) * T:(i % 2 + 1) * T]

        TDv = [pBig[3][:, 0:T].bitcast(BF16), pBig[3][:, T:2 * T].bitcast(BF16)]

        def mksem(name):
            return Sem(stack.enter_context(nc.semaphore(name)), name)

        s_t = mksem("s_t")
        s_v = mksem("s_v")
        s_a = mksem("s_a")
        s_p = mksem("s_p")
        s_x = [mksem("s_x0"), mksem("s_x1")]
        s_st = [mksem("s_st0"), mksem("s_st1")]
        s_ca = mksem("s_ca")
        s_cwin = mksem("s_cwin")
        s_cwout = mksem("s_cwout")
        s_ce = [mksem(f"s_ce{e}") for e in range(NE)]
        slot_sems = {
            "win": [mksem(f"s_win{i}") for i in range(WIN_SLOTS)],
            "w1": [mksem(f"s_w1_{i}") for i in range(W1_SLOTS)],
            "w2": [mksem(f"s_w2_{i}") for i in range(W2_SLOTS)],
            "wout": [mksem(f"s_wo{i}") for i in range(WOUT_SLOTS)],
        }
        prog = {"tensor": s_t, "vector": s_v, "scalar": s_a, "gpsimd": s_p,
                "sync": None}
        em = Emitter(prog)

        B = {}

        def buf(name):
            if name not in B:
                B[name] = Buf(name)
            return B[name]

        banks = [buf(f"bank{i}") for i in range(8)]
        win_slots = [buf(f"winS{i}") for i in range(WIN_SLOTS)]
        w1_slots = [buf(f"w1S{i}") for i in range(W1_SLOTS)]
        w2_slots = [buf(f"w2S{i}") for i in range(W2_SLOTS)]
        wout_slots = [buf(f"woutS{i}") for i in range(WOUT_SLOTS)]
        SLOTSETS = {"win": win_slots, "w1": w1_slots, "w2": w2_slots,
                    "wout": wout_slots}
        cnt = {"win": 0, "w1": 0, "w2": 0, "wout": 0}
        accB = [buf("acc0"), buf("acc1")]
        s1tB = [buf("s1t0"), buf("s1t1")]
        CONSTS = buf("consts")
        PCONSTS = buf("pconsts")

        n_tiles_total = n_tile * n_seq

        G = "gpsimd"
        em.op(G, lambda h: h.memset(cvec[:, 0:1], -0.5), writes=[CONSTS])
        em.op(G, lambda h: h.memset(cvec[:, 1:2], 1.0), writes=[CONSTS])
        em.op(G, lambda h: h.memset(cvec[:, 2:3], TWO_PI), writes=[CONSTS])
        em.op(G, lambda h: h.memset(cvec[:, 3:4], float(np.pi / 2)),
              writes=[CONSTS])
        em.op(G, lambda h: h.memset(cvec[:, 4:5], float(np.pi)),
              writes=[CONSTS])
        em.op(G, lambda h: h.memset(cvec[:, 5:6], float(-np.pi)),
              writes=[CONSTS])
        em.op(G, lambda h: h.memset(ident[:], 0.0), writes=[CONSTS])
        em.drain(G)
        em.op(G, lambda h: h.affine_select(out=ident[:], in_=ident[:],
                                           compare_op=ALU.not_equal, fill=1.0,
                                           base=0, pattern=[[-1, 128]],
                                           channel_multiplier=1),
              writes=[CONSTS])
        em.op(G, lambda h: h.memset(identf[:], 0.0), writes=[CONSTS])
        em.drain(G)
        em.op(G, lambda h: h.affine_select(out=identf[:], in_=identf[:],
                                           compare_op=ALU.not_equal, fill=1.0,
                                           base=0, pattern=[[-1, 16]],
                                           channel_multiplier=1),
              writes=[CONSTS])
        em.op(G, lambda h: h.memset(onesb[:], 1.0), writes=[CONSTS])
        em.op(G, lambda h: h.memset(carr_re[:], 0.0), writes=[buf("carr")])
        em.op(G, lambda h: h.memset(carr_im[:], 0.0), writes=[buf("carr")])
        em.op(G, lambda h: h.iota(
            idxi[:].rearrange("p (g e) -> p g e", g=4),
            pattern=[[0, 4], [1, 8]], base=0, channel_multiplier=0),
            writes=[CONSTS])
        em.drain(G)
        em.op(G, lambda h: h.tensor_copy(idxc[:], idxi[:]), writes=[CONSTS])
        em.drain(G)
        em.op(G, lambda h: h.tensor_scalar_add(idxmB[:], idxc[:], -1024.0),
              writes=[CONSTS], ms=True)

        def emit_xload(ti):
            t, sq = divmod(ti, n_seq)
            bi = ti % 2
            em.op(G, lambda h, t=t, sq=sq, bi=bi: h.dma_start(
                out=acc[:, bi],
                in_=x_d[sq, t * T:(t + 1) * T, :].rearrange(
                    "(sub p) d -> p sub d", p=128)),
                writes=[accB[bi]], dma_sem=s_x[bi])

        emit_xload(0)
        if n_tiles_total > 1:
            emit_xload(1)

        castp = {}

        def cast(name, fn, writes=(), group=None):
            gsem = group if group is not None else s_ca
            em.op(G, fn, writes=list(writes), dma_sem=gsem)
            castp[name] = gsem

        for cpc in range(C8):
            cast(f"w_in_{cpc}" if cpc else "w_in",
                 lambda h, cpc=cpc: h.dma_start(
                     out=w_ins_d[cpc],
                     in_=w_in_d.rearrange(
                         "(kk p) (c m) -> c p kk m",
                         p=128, c=8)[cpc]),
                 group=s_cwin)
        cast("wgx", lambda h: h.dma_start(
            out=wgx_sb[:],
            in_=w_gx_d.rearrange("i (kk p) m -> p i kk m", p=128)),
            writes=[CONSTS])
        cast("wga", lambda h: h.dma_start(
            out=wga_sb[:],
            in_=w_ga_d.rearrange("i (kk p) m -> p i kk m", p=128)),
            writes=[CONSTS])
        cast("rw", lambda h: h.dma_start(
            out=rw_sb[:], in_=rw_d.rearrange("(c p) e -> p c e", p=128)),
            writes=[CONSTS])
        cast("b_out", lambda h: h.dma_start(out=b_out_row[:],
                                            in_=b_out_d.unsqueeze(0)),
             writes=[CONSTS])
        cast("vst1", lambda h: h.dma_start(
            out=vstage[0:16, 0:128],
            in_=b_in_d.rearrange("(c p) -> c p", p=128)), writes=[buf("vst")])
        cast("vst2", lambda h: h.dma_start(
            out=vstage[0:8, 128:256],
            in_=b_gx_d.rearrange("(c p) -> c p", p=128)), writes=[buf("vst")])
        cast("vst3", lambda h: h.dma_start(
            out=vstage[0:8, 256:384],
            in_=b_ga_d.rearrange("(c p) -> c p", p=128)), writes=[buf("vst")])
        cast("vst4", lambda h: h.dma_start(
            out=vstage[0:8, 384:512],
            in_=arp_d.rearrange("(c p) -> c p", p=128)), writes=[buf("vst")])
        cast("vst5", lambda h: h.dma_start(
            out=vstage[0:8, 512:640],
            in_=aip_d.rearrange("(c p) -> c p", p=128)), writes=[buf("vst")])
        def emit_wout_cast():
            cast("w_out", lambda h: h.dma_start(out=w_outs_d, in_=w_out_d),
                 group=s_cwout)

        def emit_expert_casts(e, jlo=0, jhi=C11, w2=True):
            for j in range(jlo, jhi):
                for hh in range(2):
                    cast(f"w1_{e}_{j}" if hh == 0 else f"w1b_{e}_{j}",
                         lambda h, e=e, j=j, hh=hh: h.dma_start(
                             out=w1s_d[e, j][:, :, hh * 128:(hh + 1) * 128],
                             in_=w1_d[e].rearrange(
                                 "(kk p) (hh f) -> hh p kk f", p=128, hh=2)
                             [hh][:, :, j * 128:(j + 1) * 128]),
                         group=s_ce[e])
            if w2:
                cast(f"w2_{e}", lambda h, e=e: h.dma_start(out=w2s_d[e],
                                                           in_=w2_d[e]),
                     group=s_ce[e])

        DEFER_CASTS = True
        emit_expert_casts(0)
        emit_expert_casts(1)
        if not DEFER_CASTS:
            for _e in range(2, NE):
                emit_expert_casts(_e)
            emit_wout_cast()
        for bname in ("consts", "vst"):
            bb = buf(bname)
            bb.prod = (s_ca, s_ca.n)
            bb.prod_eng = "dma:s_ca"

        vec_specs = [(16, 0, b_in_sb), (8, 128, b_gx_sb), (8, 256, b_ga_sb),
                     (8, 384, vtmp_sb), (8, 512, ait_sb)]
        for nch, off, dst in vec_specs:
            em.op("tensor", lambda h, nch=nch, off=off: h.transpose(
                SBv(0)[:, 0:nch], vstage[0:nch, off:off + 128],
                identf[0:nch, 0:nch]),
                reads=[buf("vst"), CONSTS], writes=[banks[0]],
                wait=[(s_ca, s_ca.n)], ms=True)
            em.op("vector", lambda h, nch=nch, dst=dst: h.tensor_copy(
                dst[:, 0:nch], SBv(0)[:, 0:nch]),
                reads=[banks[0]], writes=[PCONSTS], ms=True)
        em.op("scalar", lambda h: h.activation(sp8_sb[:], vtmp_sb[:], AF.Exp),
              reads=[PCONSTS], writes=[PCONSTS])
        em.drain("scalar")
        em.op("scalar", lambda h: h.activation(sp8_sb[:], sp8_sb[:], AF.Ln,
                                               bias=cvec[:, 1:2]),
              reads=[CONSTS], writes=[PCONSTS], ms=True)
        em.op("vector", lambda h: h.tensor_scalar_mul(sp8_sb[:], sp8_sb[:],
                                                      -8.0),
              reads=[PCONSTS], writes=[PCONSTS])
        em.drain("vector")
        em.op("vector", lambda h: h.tensor_scalar_mul(ait_sb[:], ait_sb[:],
                                                      1.0 / TWO_PI),
              reads=[PCONSTS], writes=[PCONSTS], ms=True)

        pending = []
        pidx = [0]

        def fill(n=1):
            k = 0
            while pidx[0] < len(pending) and k < n:
                pending[pidx[0]]()
                pidx[0] += 1
                k += 1

        def fill_all():
            while pidx[0] < len(pending):
                pending[pidx[0]]()
                pidx[0] += 1

        def step(*fns):
            pending.append(lambda fns=fns: [f() for f in fns])

        def load_piece(ring, nm, src_fn, dst_fn):
            slots = SLOTSETS[ring]
            i = cnt[ring] % len(slots)
            cnt[ring] += 1
            slot = slots[i]
            w = []
            if nm in castp:
                gsem = castp[nm]
                w.append((gsem, gsem.n))
            em.op("sync", lambda h, i=i: h.dma_start(out=dst_fn(i),
                                                     in_=src_fn()),
                  writes=[slot], wait=w, dma_sem=slot_sems[ring][i])
            return i, slot

        def append_ln_steps(tj):
            bj = tj % 2
            Aj = accB[bj]
            ST = buf("stats")
            for sub in range(NSUB):
                step(lambda sub=sub, bj=bj: em.op(
                    "vector", lambda h, sub=sub, bj=bj: h.reduce_sum(
                        stats2[:, sub:sub + 1], acc[:, bj, sub, :], AX.X),
                    reads=[Aj], writes=[ST], ms=(sub == NSUB - 1)))
            for sub in range(NSUB):
                step(lambda sub=sub, bj=bj: (
                    em.drain("scalar") if sub > 0 else None,
                    em.op(
                        "scalar", lambda h, sub=sub, bj=bj: h.activation(
                            scr[:], acc[:, bj, sub, :], AF.Square,
                            accum_out=stats2[:, 4 + sub:5 + sub]),
                        reads=[Aj], writes=[buf("scr"), ST],
                        ms=(sub == NSUB - 1)))[-1])
            step(lambda: em.op("vector", lambda h: h.tensor_scalar_mul(
                stats2[:, 8:12], stats2[:, 0:4], 1.0 / D),
                reads=[ST], writes=[ST]))
            step(lambda: em.op("vector", lambda h: h.tensor_scalar_mul(
                stats2[:, 12:16], stats2[:, 4:8], 1.0 / D), writes=[ST]))
            step(lambda: em.drain("vector"),
                 lambda: em.op("vector", lambda h: h.tensor_tensor(
                     stats2[:, 16:20], stats2[:, 8:12], stats2[:, 8:12],
                     ALU.mult), writes=[ST]))
            step(lambda: em.drain("vector"),
                 lambda: em.op("vector", lambda h: h.tensor_tensor(
                     stats2[:, 20:24], stats2[:, 12:16], stats2[:, 16:20],
                     ALU.subtract), writes=[ST]))
            step(lambda: em.drain("vector"),
                 lambda: em.op("vector", lambda h: h.tensor_scalar_add(
                     stats2[:, 24:28], stats2[:, 20:24], 1e-5),
                     writes=[ST], ms=True))
            step(lambda: em.op("scalar", lambda h: h.activation(
                stats2[:, 28:32], stats2[:, 24:28], AF.Ln),
                reads=[ST], writes=[ST]))
            step(lambda: em.drain("scalar"),
                 lambda: em.op("scalar", lambda h: h.activation(
                     stats2[:, 28:32], stats2[:, 28:32], AF.Exp,
                     scale=cvec[:, 0:1]),
                     reads=[CONSTS], writes=[ST], ms=True))
            for sub in range(NSUB):
                step(lambda sub=sub, bj=bj: em.op(
                    "vector", lambda h, sub=sub, bj=bj: h.tensor_scalar(
                        xn_tok[:, sub, :], acc[:, bj, sub, :],
                        stats2[:, 8 + sub:9 + sub],
                        stats2[:, 28 + sub:29 + sub],
                        ALU.subtract, ALU.mult),
                    reads=[Aj, ST], writes=[buf("xn_tok")],
                    ms=(sub == NSUB - 1)))

        def append_scan_steps(c, sq):
            CARR = buf("carr")
            GA = buf(f"ga{c}")
            GX = buf(f"gx{c}")
            PE = "vector" if c >= 6 else "gpsimd"
            step(lambda c=c, GA=GA: em.op(
                "scalar", lambda h, c=c: h.activation(
                    bM[:], ga_all[:, c, :], AF.Exp,
                    scale=sp8_sb[:, c:c + 1]),
                reads=[GA, PCONSTS], writes=[buf("bM")], ms=True))
            step(lambda c=c, GA=GA: em.op(
                "vector", lambda h, c=c: h.tensor_scalar_mul(
                    bU2[:], ga_all[:, c, :], ait_sb[:, c:c + 1]),
                reads=[GA, PCONSTS], writes=[buf("bU2")], ms=True))
            step(lambda: em.op("vector", lambda h: h.tensor_tensor_scan(
                bL[:], bU2[:], bU2[:], 0.0, ALU.add, ALU.bypass),
                reads=[buf("bU2")], writes=[buf("bL")]),
                lambda: em.drain("vector"))
            step(lambda: em.op("vector", lambda h: h.tensor_copy(
                bI[:], bL[:]), writes=[buf("bIG")]))
            step(lambda: em.op("vector", lambda h: h.tensor_copy(
                bK[:], bI[:]), reads=[buf("bIG")], writes=[buf("bKG")]))
            step(lambda: em.op("vector", lambda h: h.tensor_tensor(
                bF[:], bL[:], bK[:], ALU.subtract),
                reads=[buf("bKG"), buf("bL")], writes=[buf("bF")], ms=True))
            step(lambda: em.op("scalar", lambda h: h.activation(
                bS[:], bF[:], AF.Sin, scale=cvec[:, 4:5]),
                reads=[buf("bF"), CONSTS], writes=[buf("bS")]))
            step(lambda: em.drain("scalar"),
                 lambda: em.op("scalar", lambda h: h.activation(
                     bC[:], bF[:], AF.Sin, scale=cvec[:, 5:6],
                     bias=cvec[:, 3:4]),
                     reads=[buf("bF"), CONSTS], writes=[buf("bC")], ms=True))
            step(lambda: em.op("vector", lambda h: h.tensor_tensor(
                bU1[:], bS[:], bC[:], ALU.mult),
                reads=[buf("bS"), buf("bC")], writes=[buf("bU1")]))
            step(lambda: em.op("vector", lambda h: h.tensor_tensor(
                bU2[:], bS[:], bS[:], ALU.mult),
                reads=[buf("bS")], writes=[buf("bU2")]))
            step(lambda: em.op("vector", lambda h: h.tensor_scalar_mul(
                bS[:], bU1[:], 2.0),
                reads=[buf("bU1")], writes=[buf("bS")]))
            step(lambda: em.op("vector", lambda h: h.tensor_scalar(
                bC[:], bU2[:], -2.0, 1.0, ALU.mult, ALU.add),
                reads=[buf("bU2")], writes=[buf("bC")], ms=True))
            step(lambda: em.op("vector", lambda h: h.tensor_tensor(
                bU2[:], bM[:], bM[:], ALU.mult),
                reads=[buf("bM")], writes=[buf("bU2")]))
            step(lambda: em.op("vector", lambda h: h.tensor_scalar(
                bU1[:], bU2[:], -1.0, 1.0, ALU.mult, ALU.add),
                reads=[buf("bU2")], writes=[buf("bU1")]))
            step(lambda: em.op("vector", lambda h: h.tensor_scalar_max(
                bU2[:], bU1[:], 1e-6),
                reads=[buf("bU1")], writes=[buf("bU2")], ms=True))
            step(lambda: em.op("scalar", lambda h: h.activation(
                bF[:], bU2[:], AF.Sqrt),
                reads=[buf("bU2")], writes=[buf("bF")], ms=True))
            step(lambda c=c, GX=GX, PE=PE: em.op(
                PE, lambda h, c=c: h.tensor_tensor(
                    bU1[:], gx_all[:, c, :], bF[:], ALU.mult),
                reads=[GX, buf("bF")], writes=[buf("bU1")]))
            step(lambda c=c, PE=PE: em.op(PE, lambda h, c=c: h.tensor_tensor(
                bWre[:], bU1[:], xhbf[:, c, :], ALU.mult),
                reads=[buf("bU1"), buf(f"xh{c}")], writes=[buf("bWre")]))
            step(lambda c=c, PE=PE: em.op(PE, lambda h, c=c: h.tensor_tensor(
                bWim[:], bU1[:], xhbf[:, c + 8, :], ALU.mult),
                reads=[buf(f"xh{c + 8}")], writes=[buf("bWim")], ms=True))
            step(lambda PE=PE: em.op(PE, lambda h: h.tensor_tensor(
                bU1[:], bWre[:], bC[:], ALU.mult),
                reads=[buf("bWre"), buf("bC")], writes=[buf("bU1")]),
                lambda PE=PE: em.op(PE, lambda h: h.tensor_tensor(
                    bU2[:], bWim[:], bS[:], ALU.mult),
                    reads=[buf("bWim"), buf("bS")], writes=[buf("bU2")]))
            step(lambda PE=PE: em.op(PE, lambda h: h.tensor_tensor(
                bGre[:], bU1[:], bU2[:], ALU.add),
                writes=[buf("bKG")]))
            step(lambda PE=PE: em.op(PE, lambda h: h.tensor_tensor(
                bU1[:], bWim[:], bC[:], ALU.mult), writes=[buf("bU1")]),
                lambda PE=PE: em.op(PE, lambda h: h.tensor_tensor(
                    bU2[:], bWre[:], bS[:], ALU.mult), writes=[buf("bU2")]))
            step(lambda PE=PE: em.op(PE, lambda h: h.tensor_tensor(
                bGim[:], bU1[:], bU2[:], ALU.subtract),
                writes=[buf("bIG")], ms=True))
            step(lambda c=c, sq=sq: em.op(
                "vector", lambda h, c=c, sq=sq: h.tensor_tensor_scan(
                    bWre[:], bM[:], bGre[:], carr_re[:, c, sq:sq + 1],
                    ALU.mult, ALU.add),
                reads=[buf("bM"), buf("bKG"), CARR], writes=[buf("bWre")]))
            step(lambda c=c, sq=sq: em.op(
                "vector", lambda h, c=c, sq=sq: h.tensor_tensor_scan(
                    bWim[:], bM[:], bGim[:], carr_im[:, c, sq:sq + 1],
                    ALU.mult, ALU.add),
                reads=[buf("bM"), buf("bIG"), CARR], writes=[buf("bWim")]),
                lambda: em.barrier("vector"))
            step(lambda PE=PE: em.op(PE, lambda h: h.tensor_tensor(
                bU1[:], bC[:], bWre[:], ALU.mult),
                reads=[buf("bC"), buf("bWre")], writes=[buf("bU1")]),
                lambda PE=PE: em.op(PE, lambda h: h.tensor_tensor(
                    bU2[:], bS[:], bWim[:], ALU.mult),
                    reads=[buf("bS"), buf("bWim")], writes=[buf("bU2")]))
            step(lambda c=c, PE=PE: em.op(PE, lambda h, c=c: h.tensor_tensor(
                hm[:, c, :], bU1[:], bU2[:], ALU.subtract),
                writes=[buf(f"hm{c}")]))
            step(lambda PE=PE: em.op(PE, lambda h: h.tensor_tensor(
                bU1[:], bC[:], bWim[:], ALU.mult), writes=[buf("bU1")]),
                lambda PE=PE: em.op(PE, lambda h: h.tensor_tensor(
                    bU2[:], bS[:], bWre[:], ALU.mult), writes=[buf("bU2")]))
            step(lambda c=c, PE=PE: em.op(PE, lambda h, c=c: h.tensor_tensor(
                hm[:, c + 8, :], bU1[:], bU2[:], ALU.add),
                writes=[buf(f"hm{c}")], ms=True))
            e511 = slice(T - 1, T)
            step(lambda PE=PE: em.op(PE, lambda h: h.tensor_tensor(
                ctmp[:, 0:1], bC[:, e511], bWre[:, e511], ALU.mult),
                writes=[buf("ctmp")]),
                lambda PE=PE: em.op(PE, lambda h: h.tensor_tensor(
                    ctmp[:, 1:2], bS[:, e511], bWim[:, e511], ALU.mult),
                    writes=[buf("ctmp")]),
                lambda PE=PE: em.op(PE, lambda h: h.tensor_tensor(
                    ctmp[:, 2:3], bC[:, e511], bWim[:, e511], ALU.mult),
                    writes=[buf("ctmp")]),
                lambda PE=PE: em.op(PE, lambda h: h.tensor_tensor(
                    ctmp[:, 3:4], bS[:, e511], bWre[:, e511], ALU.mult),
                    writes=[buf("ctmp")]),
                lambda PE=PE: em.drain(PE))
            step(lambda c=c, sq=sq, PE=PE: em.op(
                PE, lambda h, c=c, sq=sq: h.tensor_tensor(
                    carr_re[:, c, sq:sq + 1], ctmp[:, 0:1], ctmp[:, 1:2],
                    ALU.subtract), reads=[buf("ctmp")], writes=[CARR]),
                lambda c=c, sq=sq, PE=PE: em.op(
                    PE, lambda h, c=c, sq=sq: h.tensor_tensor(
                        carr_im[:, c, sq:sq + 1], ctmp[:, 2:3], ctmp[:, 3:4],
                        ALU.add), reads=[buf("ctmp")], writes=[CARR],
                    ms=True))

        bdl_parity = [0]

        def emit_bdl(c):
            bsel = bdl_parity[0] % 2
            bdl_parity[0] += 1
            bA, bB = (4, 5) if bsel == 0 else (6, 7)
            for kk in range(2):
                m = 2 * c + kk
                em.op("tensor", lambda h, c=c, kk=kk, m=m, bA=bA: h.matmul(
                    SBv(bA), wgx_sb[:, c, kk, :], xhbf[:, m, :],
                    start=(kk == 0), stop=(kk == 1)),
                    reads=[buf(f"xh{m}"), CONSTS], writes=[banks[bA]],
                    ms=(kk == 1))
            for kk in range(2):
                m = 2 * c + kk
                em.op("tensor", lambda h, c=c, kk=kk, m=m, bB=bB: h.matmul(
                    SBv(bB), wga_sb[:, c, kk, :], xhbf[:, m, :],
                    start=(kk == 0), stop=(kk == 1)),
                    reads=[buf(f"xh{m}"), CONSTS], writes=[banks[bB]],
                    ms=(kk == 1))
            em.op("scalar", lambda h, c=c, bA=bA: h.activation(
                gx_all[:, c, :], SBv(bA), AF.Sigmoid,
                bias=b_gx_sb[:, c:c + 1]),
                reads=[banks[bA], PCONSTS], writes=[buf(f"gx{c}")], ms=True)
            em.op("scalar", lambda h, c=c, bB=bB: h.activation(
                ga_all[:, c, :], SBv(bB), AF.Sigmoid,
                bias=b_ga_sb[:, c:c + 1]),
                reads=[banks[bB], PCONSTS], writes=[buf(f"ga{c}")], ms=True)

        def emit_ffn_expert(e, bi):
            A = accB[bi]
            for j in range(C11):
                si, slot = load_piece(
                    "w1", f"w1_{e}_{j}",
                    lambda e=e, j=j: w1s_d[e, j],
                    lambda i: w1_r[:, i])
                i1 = 2 * (j % 4)
                i2 = i1 + 1
                for kk in range(C8):
                    em.op("tensor", lambda h, si=si, kk=kk, i1=i1: h.matmul(
                        SBv(i1), w1_r[:, si, kk, 0:128], xnf[:, kk, :],
                        start=(kk == 0), stop=(kk == C8 - 1)),
                        reads=[slot, buf("xnf")], writes=[banks[i1]],
                        ms=(kk == C8 - 1))
                for kk in range(C8):
                    em.op("tensor", lambda h, si=si, kk=kk, i2=i2: h.matmul(
                        SBv(i2), w1_r[:, si, kk, 128:256], xnf[:, kk, :],
                        start=(kk == 0), stop=(kk == C8 - 1)),
                        reads=[slot, buf("xnf")], writes=[banks[i2]],
                        ms=(kk == C8 - 1))
                sB = s1tB[j % 2]
                em.op("scalar", lambda h, j=j, i1=i1: h.activation(
                    s1t[:, j % 2, :], SBv(i1), AF.Silu),
                    reads=[banks[i1]], writes=[sB], ms=True)
                em.op("vector", lambda h, j=j, i2=i2: h.tensor_tensor(
                    g_sb[:, j, :], s1t[:, j % 2, :], SBv(i2), ALU.mult),
                    reads=[sB, banks[i2]], writes=[buf("g_sb")], ms=True)
                fill(1)
            for kk in range(C11):
                si, slot = load_piece(
                    "w2", f"w2_{e}",
                    lambda e=e, kk=kk: w2s_d[e, kk * 128:(kk + 1) * 128, :],
                    lambda i: w2_r[:, i])
                for tsub in range(NSUB):
                    for hh in range(2):
                        em.op("tensor",
                              lambda h, si=si, kk=kk, tsub=tsub, hh=hh:
                              h.matmul(
                                  pBig[tsub][:, hh * T:(hh + 1) * T],
                                  g_sb[:, kk, tsub * 128:(tsub + 1) * 128],
                                  w2_r[:, si, hh * T:(hh + 1) * T],
                                  start=(kk == 0), stop=(kk == C11 - 1)),
                              reads=[slot, buf("g_sb")],
                              writes=[banks[2 * tsub + hh]],
                              ms=(kk == C11 - 1 or
                                  (tsub == NSUB - 1 and hh == 1)))
                fill(2)
            for bank in range(8):
                tsub, hh = bank // 2, bank % 2
                em.op("vector", lambda h, tsub=tsub, hh=hh, e=e, bi=bi:
                      h.scalar_tensor_tensor(
                          acc[:, bi, tsub, hh * T:(hh + 1) * T],
                          pBig[tsub][:, hh * T:(hh + 1) * T],
                          lp_sb[:, tsub * 8 + e:tsub * 8 + e + 1],
                          acc[:, bi, tsub, hh * T:(hh + 1) * T],
                          ALU.mult, ALU.add),
                      reads=[banks[bank], buf("lp"), A],
                      writes=[A], ms=True)
            fill(3)

        def emit_attn_out(bi):
            A = accB[bi]
            kks = [c + 8 * q for c in range(C8) for q in range(2)]
            for ki, kk in enumerate(kks):
                si, slot = load_piece(
                    "wout", "w_out",
                    lambda kk=kk: w_outs_d.rearrange(
                        "(kk p) m -> kk p m", p=128)[kk],
                    lambda i: wout_r[:, i])
                for tsub in range(NSUB):
                    for hh in range(2):
                        em.op("tensor",
                              lambda h, si=si, kk=kk, tsub=tsub, hh=hh,
                              ki=ki:
                              h.matmul(
                                  pBig[tsub][:, hh * T:(hh + 1) * T],
                                  hm[:, kk, tsub * 128:(tsub + 1) * 128],
                                  wout_r[:, si, hh * T:(hh + 1) * T],
                                  start=(ki == 0), stop=False),
                              reads=[slot, buf(f"hm{kk % 8}")],
                              writes=[banks[2 * tsub + hh]],
                              ms=(tsub == NSUB - 1 and hh == 1))
            for tsub in range(NSUB):
                for hh in range(2):
                    em.op("tensor", lambda h, tsub=tsub, hh=hh: h.matmul(
                        pBig[tsub][:, hh * T:(hh + 1) * T],
                        onesb[0:1, 0:128],
                        b_out_row[0:1, hh * T:(hh + 1) * T],
                        start=False, stop=True),
                        reads=[CONSTS],
                        writes=[banks[2 * tsub + hh]],
                        ms=(hh == 1))
            if debug and bi == 0 and not dbg_done:
                dbg_done.append(1)
                em.op("vector", lambda h: h.tensor_copy(bL[:], pBig[0][:, 0:T]),
                      reads=[banks[0]], writes=[buf("bL")], ms=True)
                em.op(G, lambda h: h.dma_start(out=dbg_ps, in_=bL[:]),
                      reads=[buf("bL")], dma_sem=s_st[0])
            for tsub in (3, 0, 1, 2):
                for hh in range(2):
                    em.op("vector", lambda h, tsub=tsub, hh=hh, bi=bi:
                          h.tensor_tensor(
                              acc[:, bi, tsub, hh * T:(hh + 1) * T],
                              acc[:, bi, tsub, hh * T:(hh + 1) * T],
                              pBig[tsub][:, hh * T:(hh + 1) * T], ALU.add),
                          reads=[banks[2 * tsub + hh], A],
                          writes=[A], ms=True)
            if debug and bi == 0 and len(dbg_done) == 1:
                dbg_done.append(2)
                em.op(G, lambda h: h.dma_start(out=dbg_ac2, in_=acc[:, 0]),
                      reads=[A], dma_sem=s_st[0])
                em.op(G, lambda h: h.dma_start(out=dbg_hm2, in_=hm[:]),
                      reads=[buf("hm")], dma_sem=s_st[0])

        def emit_router():
            rb = banks[4]
            for tsub in range(NSUB):
                for c in range(C8):
                    em.op("tensor", lambda h, tsub=tsub, c=c: h.matmul(
                        SBv(4)[:, tsub * 8:(tsub + 1) * 8],
                        xnf[:, c, tsub * 128:(tsub + 1) * 128],
                        rw_sb[:, c, :], start=(c == 0), stop=(c == C8 - 1)),
                        reads=[buf("xnf"), CONSTS], writes=[rb],
                        ms=(c == C8 - 1))
            em.op("vector", lambda h: h.tensor_copy(lg[:], SBv(4)[:, 0:32]),
                  reads=[rb], writes=[buf("lg")], ms=True)

            def r3(ap):
                return ap.rearrange("p (gr e) -> p gr e", gr=4)

            def b4(ap):
                return ap.unsqueeze(2).broadcast_to([128, 4, 8])

            V = "vector"
            em.drain(V)
            em.op(V, lambda h: h.tensor_reduce(
                rm4[:], r3(lg[:]), AX.X, ALU.max), reads=[buf("lg")],
                writes=[buf("rm4")])
            em.drain(V)
            em.op(V, lambda h: h.tensor_tensor(
                r3(rex[:]), r3(lg[:]), b4(rm4[:]), ALU.subtract),
                reads=[buf("lg"), buf("rm4")], writes=[buf("rex")], ms=True)
            em.op("scalar", lambda h: h.activation(rex[:], rex[:], AF.Exp),
                  reads=[buf("rex")], writes=[buf("rex")], ms=True)
            em.drain(V)
            em.op(V, lambda h: h.tensor_reduce(
                rs4[:], r3(rex[:]), AX.X, ALU.add), reads=[buf("rex")],
                writes=[buf("rs4")])
            em.drain(V)
            em.op(V, lambda h: h.reciprocal(rs4[:], rs4[:]),
                  writes=[buf("rs4")])
            em.drain(V)
            em.op(V, lambda h: h.tensor_tensor(
                r3(rpr[:]), r3(rex[:]), b4(rs4[:]), ALU.mult),
                reads=[buf("rex"), buf("rs4")], writes=[buf("rpr")])
            em.drain(V)
            em.op(V, lambda h: h.tensor_reduce(
                rv4[:], r3(rpr[:]), AX.X, ALU.max), reads=[buf("rpr")],
                writes=[buf("rv4")])
            em.drain(V)
            em.op(V, lambda h: h.tensor_tensor(
                r3(rt1[:]), r3(rpr[:]), b4(rv4[:]), ALU.is_equal),
                reads=[buf("rpr"), buf("rv4")], writes=[buf("rt1")])
            em.drain(V)
            em.op(V, lambda h: h.tensor_tensor(
                rt2[:], rt1[:], idxmB[:], ALU.mult),
                reads=[buf("rt1"), CONSTS], writes=[buf("rt2")])
            em.drain(V)
            em.op(V, lambda h: h.tensor_scalar_add(
                rt3[:], rt2[:], 1024.0), reads=[buf("rt2")],
                writes=[buf("rt3")])
            em.drain(V)
            em.op(V, lambda h: h.tensor_reduce(
                rv4[:], r3(rt3[:]), AX.X, ALU.min), reads=[buf("rt3")],
                writes=[buf("rv4")])
            em.drain(V)
            em.op(V, lambda h: h.tensor_tensor(
                r3(rt1[:]), r3(idxc[:]), b4(rv4[:]), ALU.is_equal),
                reads=[CONSTS, buf("rv4")], writes=[buf("rt1")])
            em.drain(V)
            em.op(V, lambda h: h.scalar_tensor_tensor(
                rt2[:], rt1[:], -1e9, rpr[:], ALU.mult, ALU.add),
                reads=[buf("rt1"), buf("rpr")], writes=[buf("rt2")])
            em.drain(V)
            em.op(V, lambda h: h.tensor_reduce(
                rv4[:], r3(rt2[:]), AX.X, ALU.max), reads=[buf("rt2")],
                writes=[buf("rv4")])
            em.drain(V)
            em.op(V, lambda h: h.tensor_tensor(
                r3(rt3[:]), r3(rt2[:]), b4(rv4[:]), ALU.is_equal),
                reads=[buf("rt2"), buf("rv4")], writes=[buf("rt3")])
            em.drain(V)
            em.op(V, lambda h: h.tensor_tensor(
                rt2[:], rt3[:], idxmB[:], ALU.mult),
                reads=[buf("rt3"), CONSTS], writes=[buf("rt2")])
            em.drain(V)
            em.op(V, lambda h: h.tensor_scalar_add(
                rt3[:], rt2[:], 1024.0), reads=[buf("rt2")],
                writes=[buf("rt3")])
            em.drain(V)
            em.op(V, lambda h: h.tensor_reduce(
                rv4[:], r3(rt3[:]), AX.X, ALU.min), reads=[buf("rt3")],
                writes=[buf("rv4")])
            em.drain(V)
            em.op(V, lambda h: h.tensor_tensor(
                r3(rt2[:]), r3(idxc[:]), b4(rv4[:]), ALU.is_equal),
                reads=[CONSTS, buf("rv4")], writes=[buf("rt2")])
            em.drain(V)
            em.op(V, lambda h: h.tensor_tensor(
                rt3[:], rt1[:], rt2[:], ALU.add),
                reads=[buf("rt1"), buf("rt2")], writes=[buf("rt3")])
            em.drain(V)
            em.op(V, lambda h: h.tensor_tensor(
                lp_sb[:], rpr[:], rt3[:], ALU.mult),
                reads=[buf("rpr"), buf("rt3")], writes=[buf("lp")], ms=True)

        dbg_done = []
        append_ln_steps(0)
        fill_all()

        for ti in range(n_tiles_total):
            t, sq = divmod(ti, n_seq)
            bi = ti % 2
            A = accB[bi]
            del pending[:]
            pidx[0] = 0

            for c in range(C8):
                td = TDv[c % 2]
                bk = banks[6 + c % 2]
                for sub in range(NSUB):
                    em.op("tensor", lambda h, c=c, sub=sub, td=td: h.transpose(
                        td[:, sub * 128:(sub + 1) * 128],
                        xn_tok[:, sub, c * 128:(c + 1) * 128], ident[:]),
                        reads=[buf("xn_tok"), CONSTS], writes=[bk],
                        ms=(sub == NSUB - 1))
                em.op("vector", lambda h, c=c, td=td: h.tensor_copy(
                    xnf[:, c, :], td[:, 0:T]),
                    reads=[bk], writes=[buf("xnf")], ms=True)

            emit_router()

            hidx = 0
            for piece in range(C8):
                si, slot = load_piece(
                    "win", "w_in",
                    lambda piece=piece: w_ins_d[piece],
                    lambda i: win_r[:, i])
                for half in range(2):
                    m = 2 * piece + half
                    bk_i = hidx % 4
                    hidx += 1
                    for kk in range(C8):
                        em.op("tensor",
                              lambda h, si=si, kk=kk, half=half, bk_i=bk_i:
                              h.matmul(SBv(bk_i),
                                       win_r[:, si, kk,
                                             half * 128:(half + 1) * 128],
                                       xnf[:, kk, :], start=(kk == 0),
                                       stop=(kk == C8 - 1)),
                              reads=[slot, buf("xnf")], writes=[banks[bk_i]],
                              ms=(kk == C8 - 1))
                    em.op("vector", lambda h, m=m, bk_i=bk_i: h.tensor_scalar_add(
                        xhbf[:, m, :], SBv(bk_i), b_in_sb[:, m:m + 1]),
                        reads=[banks[bk_i], PCONSTS], writes=[buf(f"xh{m}")],
                        ms=True)
                    fill(1)
                emit_bdl(piece)
                if piece >= 4:
                    for c in (2 * (piece - 4), 2 * (piece - 4) + 1):
                        append_scan_steps(c, sq)
                fill(1)

            for e in range(NE):
                if ti == 0 and DEFER_CASTS:
                    if e < 6:
                        emit_expert_casts(e + 2, 0, 6, w2=False)
                    if 1 <= e < 7:
                        emit_expert_casts(e + 1, 6, C11, w2=True)
                    if e == 4:
                        emit_wout_cast()
                if e == 2 and ti + 1 < n_tiles_total:
                    append_ln_steps(ti + 1)
                emit_ffn_expert(e, bi)

            fill_all()
            if debug and ti == 0:
                em.op(G, lambda h: h.dma_start(out=dbg_xh, in_=xhbf[:]),
                      reads=[buf(f"xh{m}") for m in range(C16)], dma_sem=s_st[0])
                em.op(G, lambda h: h.dma_start(out=dbg_gx, in_=gx_all[:]),
                      reads=[buf(f"gx{c}") for c in range(C8)], dma_sem=s_st[0])
                em.op(G, lambda h: h.dma_start(out=dbg_ga, in_=ga_all[:]),
                      reads=[buf(f"ga{c}") for c in range(C8)], dma_sem=s_st[0])
                em.op(G, lambda h: h.dma_start(out=dbg_hm, in_=hm[:]),
                      reads=[buf("hm")] + [buf(f"hm{c}") for c in range(C8)], dma_sem=s_st[0])
                em.op(G, lambda h: h.dma_start(out=dbg_xn, in_=xnf[:]),
                      reads=[buf("xnf")], dma_sem=s_st[0])
                em.op(G, lambda h, bi=bi: h.dma_start(out=dbg_ac, in_=acc[:, bi]),
                      reads=[accB[bi]], dma_sem=s_st[0])
                em.op(G, lambda h: h.dma_start(out=dbg_lp, in_=lp_sb[:]),
                      reads=[buf("lp")], dma_sem=s_st[0])
                em.op(G, lambda h: h.dma_start(out=dbg_wo, in_=w_outs_d),
                      wait=[(s_cwout, s_cwout.n)], dma_sem=s_st[0])
                em.op(G, lambda h: h.dma_start(out=dbg_wi, in_=w_ins_d),
                      wait=[(s_cwin, s_cwin.n)], dma_sem=s_st[0])
            emit_attn_out(bi)

            em.op(G, lambda h, t=t, sq=sq, bi=bi: h.dma_start(
                out=out_d[sq, t * T:(t + 1) * T, :].rearrange(
                    "(sub p) d -> p sub d", p=128),
                in_=acc[:, bi]), reads=[A], dma_sem=s_st[bi])
            if ti + 2 < n_tiles_total:
                emit_xload(ti + 2)

        em.raw(G, lambda h: h.nop(),
               wait=[(s_st[0], s_st[0].n), (s_st[1], s_st[1].n)])

        counts = {k.name: k.n for k in (s_t, s_v, s_a, s_p)}
        n_instr = sum(len(v) for v in em.cmds.values())
        print(f"[build] instrs={n_instr} sems={counts}")

        blk = stack.enter_context(nc.Block())

        @blk.sync
        def _(h):
            for cmd in em.cmds["sync"]:
                cmd(h)

        @blk.gpsimd
        def _(h):
            for cmd in em.cmds["gpsimd"]:
                cmd(h)

        @blk.scalar
        def _(h):
            for cmd in em.cmds["scalar"]:
                cmd(h)

        @blk.vector
        def _(h):
            for cmd in em.cmds["vector"]:
                cmd(h)

        @blk.tensor
        def _(h):
            for cmd in em.cmds["tensor"]:
                cmd(h)

    return nc


_NC_CACHE = {}


def _get_program(n_seq, n_tile):
    key = (n_seq, n_tile)
    if key not in _NC_CACHE:
        _NC_CACHE[key] = build_program(n_seq, n_tile)
    return _NC_CACHE[key]


def make_in_maps(x, weights, n_cores):
    per = x.shape[0] // n_cores
    in_maps = []
    for c in range(n_cores):
        m = {"x": np.ascontiguousarray(x[per * c:per * (c + 1)])}
        m.update(weights)
        in_maps.append(m)
    return in_maps


def kernel(L0_attn_x, L0_attn_w_in, L0_attn_b_in, L0_attn_w_gx, L0_attn_b_gx,
           L0_attn_w_ga, L0_attn_b_ga, L0_attn_a_real_param,
           L0_attn_a_imag_param, L0_attn_w_out, L0_attn_b_out,
           L0_ffn_router_weight, L0_ffn_w1, L0_ffn_w2):
    x = np.ascontiguousarray(np.asarray(L0_attn_x, dtype=np.float32))
    weights = dict(
        w_in=np.ascontiguousarray(np.asarray(L0_attn_w_in, np.float32)),
        b_in=np.ascontiguousarray(np.asarray(L0_attn_b_in, np.float32)),
        w_gx=np.ascontiguousarray(np.asarray(L0_attn_w_gx, np.float32)),
        b_gx=np.ascontiguousarray(np.asarray(L0_attn_b_gx, np.float32)),
        w_ga=np.ascontiguousarray(np.asarray(L0_attn_w_ga, np.float32)),
        b_ga=np.ascontiguousarray(np.asarray(L0_attn_b_ga, np.float32)),
        arp=np.ascontiguousarray(np.asarray(L0_attn_a_real_param, np.float32)),
        aip=np.ascontiguousarray(np.asarray(L0_attn_a_imag_param, np.float32)),
        w_out=np.ascontiguousarray(np.asarray(L0_attn_w_out, np.float32)),
        b_out=np.ascontiguousarray(np.asarray(L0_attn_b_out, np.float32)),
        rw=np.ascontiguousarray(np.asarray(L0_ffn_router_weight, np.float32)),
        w1=np.ascontiguousarray(np.asarray(L0_ffn_w1, np.float32)),
        w2=np.ascontiguousarray(np.asarray(L0_ffn_w2, np.float32)),
    )
    nc = _get_program(2, 8)
    in_maps = make_in_maps(x, weights, 8)
    res = run_bass_kernel_spmd(nc, in_maps, core_ids=list(range(8)))
    return np.concatenate([r["out"] for r in res.results], axis=0)


# version 16
# speedup vs baseline: 1.0890x; 1.0890x over previous
import numpy as np
from contextlib import ExitStack

import concourse.bass as bass
import concourse.mybir as mybir
import concourse.bass_utils as _bass_utils
from concourse.bass_utils import run_bass_kernel_spmd

if not getattr(_bass_utils, "_ldwopt_patched", False):
    _orig_run_command = _bass_utils.run_command

    def _run_command_ldwopt(argv, **kwargs):
        argv = ["--enable-ldw-opt=true" if a == "--enable-ldw-opt=false" else a
                for a in argv]
        return _orig_run_command(argv, **kwargs)

    _bass_utils.run_command = _run_command_ldwopt
    _bass_utils._ldwopt_patched = True

F32 = mybir.dt.float32
BF16 = mybir.dt.bfloat16
I32 = mybir.dt.int32
AF = mybir.ActivationFunctionType
ALU = mybir.AluOpType
AX = mybir.AxisListType

D = 1024
DH = 2048
NE = 8
FF = 1408
T = 512
NSUB = 4
C8 = 8
C16 = 16
C11 = 11
TWO_PI = float(2.0 * np.pi)

ENGS = ("sync", "gpsimd", "scalar", "vector", "tensor")


class Sem:
    def __init__(self, h, name):
        self.h = h
        self.n = 0
        self.name = name


class Buf:
    __slots__ = ("name", "prod", "prod_eng", "readers")

    def __init__(self, name):
        self.name = name
        self.prod = None
        self.prod_eng = None
        self.readers = {}


class Emitter:
    def __init__(self, prog_sems):
        self.cmds = {e: [] for e in ENGS}
        self.waited = {e: {} for e in ENGS}
        self.prog = prog_sems
        self.pend_r = {e: set() for e in ENGS}
        self.pend_w = {e: set() for e in ENGS}

    def raw(self, eng, fn, wait=(), inc=()):
        w = []
        for s, v in wait:
            if v <= 0 or self.waited[eng].get(id(s), -1) >= v:
                continue
            self.waited[eng][id(s)] = v
            w.append((s, v))
        for s, a in inc:
            s.n += a

        def cmd(h, fn=fn, w=tuple(w), inc=tuple(inc)):
            for s, v in w:
                h.wait_ge(s.h, v)
            ins = fn(h)
            for s, a in inc:
                ins = ins.then_inc(s.h, a)
        self.cmds[eng].append(cmd)

    def drain(self, eng):
        self.cmds[eng].append(lambda h: h.drain())

    def barrier(self, eng):
        self.drain(eng)
        sem = self.prog[eng]
        sem.n += 1
        self.cmds[eng].append(lambda h, sem=sem: h.nop().then_inc(sem.h, 1))
        self.cover(eng)

    def op(self, eng, fn, reads=(), writes=(), wait=(), inc=(), ms=False,
           dma_sem=None):
        waits = list(wait)
        for b in reads:
            for e2 in ENGS:
                if e2 != eng:
                    assert b not in self.pend_w[e2], \
                        f"{eng} reads {b.name} pending-written by {e2}"
            if b.prod is not None and b.prod_eng != eng:
                waits.append(b.prod)
        for b in writes:
            for e2 in ENGS:
                if e2 != eng:
                    assert b not in self.pend_w[e2], \
                        f"{eng} writes {b.name} pending-written by {e2}"
                    assert b not in self.pend_r[e2], \
                        f"{eng} writes {b.name} pending-read by {e2}"
            for k, sv in b.readers.items():
                if k != eng:
                    waits.append(sv)
            if b.prod is not None and b.prod_eng != eng:
                waits.append(b.prod)

        incs = list(inc)
        soft_ms = ms and eng in ("vector", "scalar", "gpsimd")
        if ms and not soft_ms:
            incs.append((self.prog[eng], 1))
        if dma_sem is not None:
            incs.append((dma_sem, 16))
        self.raw(eng, fn, wait=waits, inc=incs)

        if dma_sem is not None:
            key = "dma:" + dma_sem.name
            for b in writes:
                b.prod = (dma_sem, dma_sem.n)
                b.prod_eng = key
                b.readers = {}
            for b in reads:
                b.readers[key] = (dma_sem, dma_sem.n)
            return
        for b in reads:
            self.pend_r[eng].add(b)
        for b in writes:
            self.pend_w[eng].add(b)
        if ms:
            if soft_ms:
                self.barrier(eng)
            else:
                self.cover(eng)

    def cover(self, eng):
        s = self.prog[eng]
        for b in self.pend_r[eng]:
            b.readers[eng] = (s, s.n)
        for b in self.pend_w[eng]:
            b.prod = (s, s.n)
            b.prod_eng = eng
            b.readers = {k: v for k, v in b.readers.items() if k == eng}
            b.readers[eng] = (s, s.n)
        self.pend_r[eng].clear()
        self.pend_w[eng].clear()


def build_program(n_seq=2, n_tile=8, debug=False):
    S = n_tile * T
    nc = bass.Bass("TRN2", target_bir_lowering=False)

    x_d = nc.dram_tensor("x", [n_seq, S, D], F32, kind="ExternalInput").ap()
    w_in_d = nc.dram_tensor("w_in", [D, DH], F32, kind="ExternalInput").ap()
    b_in_d = nc.dram_tensor("b_in", [DH], F32, kind="ExternalInput").ap()
    w_gx_d = nc.dram_tensor("w_gx", [NE, 256, 128], F32, kind="ExternalInput").ap()
    b_gx_d = nc.dram_tensor("b_gx", [D], F32, kind="ExternalInput").ap()
    w_ga_d = nc.dram_tensor("w_ga", [NE, 256, 128], F32, kind="ExternalInput").ap()
    b_ga_d = nc.dram_tensor("b_ga", [D], F32, kind="ExternalInput").ap()
    arp_d = nc.dram_tensor("arp", [D], F32, kind="ExternalInput").ap()
    aip_d = nc.dram_tensor("aip", [D], F32, kind="ExternalInput").ap()
    w_out_d = nc.dram_tensor("w_out", [DH, D], F32, kind="ExternalInput").ap()
    b_out_d = nc.dram_tensor("b_out", [D], F32, kind="ExternalInput").ap()
    rw_d = nc.dram_tensor("rw", [D, NE], F32, kind="ExternalInput").ap()
    w1_d = nc.dram_tensor("w1", [NE, D, 2 * FF], F32, kind="ExternalInput").ap()
    w2_d = nc.dram_tensor("w2", [NE, FF, D], F32, kind="ExternalInput").ap()
    out_d = nc.dram_tensor("out", [n_seq, S, D], F32, kind="ExternalOutput").ap()
    if debug:
        dbg_xh = nc.dram_tensor("dbg_xh", [128, C16, T], BF16, kind="ExternalOutput").ap()
        dbg_gx = nc.dram_tensor("dbg_gx", [128, C8, T], F32, kind="ExternalOutput").ap()
        dbg_ga = nc.dram_tensor("dbg_ga", [128, C8, T], F32, kind="ExternalOutput").ap()
        dbg_hm = nc.dram_tensor("dbg_hm", [128, C16, T], BF16, kind="ExternalOutput").ap()
        dbg_xn = nc.dram_tensor("dbg_xn", [128, C8, T], BF16, kind="ExternalOutput").ap()
        dbg_ac = nc.dram_tensor("dbg_ac", [128, NSUB, D], F32, kind="ExternalOutput").ap()
        dbg_wo = nc.dram_tensor("dbg_wo", [DH, D], BF16, kind="ExternalOutput").ap()
        dbg_ps = nc.dram_tensor("dbg_ps", [128, T], F32, kind="ExternalOutput").ap()
        dbg_ac2 = nc.dram_tensor("dbg_ac2", [128, NSUB, D], F32, kind="ExternalOutput").ap()
        dbg_hm2 = nc.dram_tensor("dbg_hm2", [128, C16, T], BF16, kind="ExternalOutput").ap()
        dbg_wi = nc.dram_tensor("dbg_wi", [C8, 128, C8, 256], BF16, kind="ExternalOutput").ap()
        dbg_lp = nc.dram_tensor("dbg_lp", [128, 32], F32, kind="ExternalOutput").ap()

    w_ins_d = nc.dram_tensor("w_ins", [C8, 128, C8, 256], BF16, kind="Internal").ap()
    w_outs_d = nc.dram_tensor("w_outs", [DH, D], BF16, kind="Internal").ap()
    w1s_d = nc.dram_tensor("w1s", [NE, C11, 128, C8, 256], BF16, kind="Internal").ap()
    w2s_d = nc.dram_tensor("w2s", [NE, FF, D], BF16, kind="Internal").ap()

    with ExitStack() as stack:
        def sbt(name, shape, dt=F32):
            return stack.enter_context(nc.sbuf_tensor(name, shape, dt))

        def pst(name, shape, dt=F32):
            return stack.enter_context(nc.psum_tensor(name, shape, dt))

        ident = sbt("ident", [128, 128], BF16)
        identf = sbt("identf", [16, 16], F32)
        onesb = sbt("onesb", [1, T], BF16)
        cvec = sbt("cvec", [128, 4])
        b_in_sb = sbt("b_in_sb", [128, C16])
        b_gx_sb = sbt("b_gx_sb", [128, C8])
        b_ga_sb = sbt("b_ga_sb", [128, C8])
        sp8_sb = sbt("sp8_sb", [128, C8])
        ait_sb = sbt("ait_sb", [128, C8])
        vtmp_sb = sbt("vtmp_sb", [128, C8])
        b_out_row = sbt("b_out_row", [1, D], BF16)
        rw_sb = sbt("rw_sb", [128, C8, NE], BF16)
        wgx_sb = sbt("wgx_sb", [128, C8, 2, 128], BF16)
        wga_sb = sbt("wga_sb", [128, C8, 2, 128], BF16)
        vstage = sbt("vstage", [16, 128 * 5])
        carr_re = sbt("carr_re", [128, C8, n_seq])
        carr_im = sbt("carr_im", [128, C8, n_seq])
        idxc = sbt("idxc", [128, 32])
        idxmB = sbt("idxmB", [128, 32])
        idxi = sbt("idxi", [128, 32], I32)

        WIN_SLOTS, W1_SLOTS, W2_SLOTS, WOUT_SLOTS = 2, 4, 4, 4
        win_r = sbt("win_r", [128, WIN_SLOTS, C8, 256], BF16)
        w1_r = sbt("w1_r", [128, W1_SLOTS, C8, 256], BF16)
        w2_r = sbt("w2_r", [128, W2_SLOTS, D], BF16)
        wout_r = sbt("wout_r", [128, WOUT_SLOTS, D], BF16)

        acc = sbt("acc", [128, 2, NSUB, D])
        xn_tok = sbt("xn_tok", [128, NSUB, D], BF16)
        xnf = sbt("xnf", [128, C8, T], BF16)
        xhbf = sbt("xhbf", [128, C16, T], BF16)
        hm = sbt("hm", [128, C16, T], BF16)
        gx_all = sbt("gx_all", [128, C8, T], BF16)
        ga_all = sbt("ga_all", [128, C8, T])
        g_sb = sbt("g_sb", [128, C11, T], BF16)
        s1t = sbt("s1t", [128, 2, T], BF16)
        scr = sbt("scr", [128, D], BF16)
        stats2 = sbt("stats2", [128, 32])
        lg = sbt("lg", [128, 32])
        rex = sbt("rex", [128, 32])
        rpr = sbt("rpr", [128, 32])
        rt1 = sbt("rt1", [128, 32])
        rt2 = sbt("rt2", [128, 32])
        rt3 = sbt("rt3", [128, 32])
        lp_sb = sbt("lp_sb", [128, 32])
        rm4 = sbt("rm4", [128, 4])
        rs4 = sbt("rs4", [128, 4])
        rv4 = sbt("rv4", [128, 4])
        bL = sbt("bL", [128, T])
        bF = sbt("bF", [128, T])
        bM = sbt("bM", [128, T])
        bS = sbt("bS", [128, T])
        bC = sbt("bC", [128, T])
        bU1 = sbt("bU1", [128, T])
        bU2 = sbt("bU2", [128, T])
        bWre = sbt("bWre", [128, T])
        bWim = sbt("bWim", [128, T])
        bKGre = sbt("bKGre", [128, T])
        bIGim = sbt("bIGim", [128, T])
        bK = bKGre
        bGre = bKGre
        bGim = bIGim
        bI = bIGim[:].bitcast(I32)
        ctmp = sbt("ctmp", [128, 4])

        pBig = [pst(f"pBig{i}", [128, D]) for i in range(4)]

        def SBv(i):
            return pBig[i // 2][:, (i % 2) * T:(i % 2 + 1) * T]

        TDv = [pBig[3][:, 0:T].bitcast(BF16), pBig[3][:, T:2 * T].bitcast(BF16)]

        def mksem(name):
            return Sem(stack.enter_context(nc.semaphore(name)), name)

        s_t = mksem("s_t")
        s_v = mksem("s_v")
        s_a = mksem("s_a")
        s_p = mksem("s_p")
        s_x = [mksem("s_x0"), mksem("s_x1")]
        s_st = [mksem("s_st0"), mksem("s_st1")]
        s_ca = mksem("s_ca")
        s_cwin = mksem("s_cwin")
        s_cwout = mksem("s_cwout")
        s_ce = [mksem(f"s_ce{e}") for e in range(NE)]
        slot_sems = {
            "win": [mksem(f"s_win{i}") for i in range(WIN_SLOTS)],
            "w1": [mksem(f"s_w1_{i}") for i in range(W1_SLOTS)],
            "w2": [mksem(f"s_w2_{i}") for i in range(W2_SLOTS)],
            "wout": [mksem(f"s_wo{i}") for i in range(WOUT_SLOTS)],
        }
        prog = {"tensor": s_t, "vector": s_v, "scalar": s_a, "gpsimd": s_p,
                "sync": None}
        em = Emitter(prog)

        B = {}

        def buf(name):
            if name not in B:
                B[name] = Buf(name)
            return B[name]

        banks = [buf(f"bank{i}") for i in range(8)]
        win_slots = [buf(f"winS{i}") for i in range(WIN_SLOTS)]
        w1_slots = [buf(f"w1S{i}") for i in range(W1_SLOTS)]
        w2_slots = [buf(f"w2S{i}") for i in range(W2_SLOTS)]
        wout_slots = [buf(f"woutS{i}") for i in range(WOUT_SLOTS)]
        SLOTSETS = {"win": win_slots, "w1": w1_slots, "w2": w2_slots,
                    "wout": wout_slots}
        cnt = {"win": 0, "w1": 0, "w2": 0, "wout": 0}
        accB = [buf("acc0"), buf("acc1")]
        s1tB = [buf("s1t0"), buf("s1t1")]
        CONSTS = buf("consts")
        PCONSTS = buf("pconsts")

        n_tiles_total = n_tile * n_seq

        G = "gpsimd"
        em.op(G, lambda h: h.memset(cvec[:, 0:1], -0.5), writes=[CONSTS])
        em.op(G, lambda h: h.memset(cvec[:, 1:2], 1.0), writes=[CONSTS])
        em.op(G, lambda h: h.memset(cvec[:, 2:3], TWO_PI), writes=[CONSTS])
        em.op(G, lambda h: h.memset(ident[:], 0.0), writes=[CONSTS])
        em.drain(G)
        em.op(G, lambda h: h.affine_select(out=ident[:], in_=ident[:],
                                           compare_op=ALU.not_equal, fill=1.0,
                                           base=0, pattern=[[-1, 128]],
                                           channel_multiplier=1),
              writes=[CONSTS])
        em.op(G, lambda h: h.memset(identf[:], 0.0), writes=[CONSTS])
        em.drain(G)
        em.op(G, lambda h: h.affine_select(out=identf[:], in_=identf[:],
                                           compare_op=ALU.not_equal, fill=1.0,
                                           base=0, pattern=[[-1, 16]],
                                           channel_multiplier=1),
              writes=[CONSTS])
        em.op(G, lambda h: h.memset(onesb[:], 1.0), writes=[CONSTS])
        em.op(G, lambda h: h.memset(carr_re[:], 0.0), writes=[buf("carr")])
        em.op(G, lambda h: h.memset(carr_im[:], 0.0), writes=[buf("carr")])
        em.op(G, lambda h: h.iota(
            idxi[:].rearrange("p (g e) -> p g e", g=4),
            pattern=[[0, 4], [1, 8]], base=0, channel_multiplier=0),
            writes=[CONSTS])
        em.drain(G)
        em.op(G, lambda h: h.tensor_copy(idxc[:], idxi[:]), writes=[CONSTS])
        em.drain(G)
        em.op(G, lambda h: h.tensor_scalar_add(idxmB[:], idxc[:], -1024.0),
              writes=[CONSTS], ms=True)

        def emit_xload(ti):
            t, sq = divmod(ti, n_seq)
            bi = ti % 2
            em.op(G, lambda h, t=t, sq=sq, bi=bi: h.dma_start(
                out=acc[:, bi],
                in_=x_d[sq, t * T:(t + 1) * T, :].rearrange(
                    "(sub p) d -> p sub d", p=128)),
                writes=[accB[bi]], dma_sem=s_x[bi])

        emit_xload(0)
        if n_tiles_total > 1:
            emit_xload(1)

        castp = {}

        def cast(name, fn, writes=(), group=None):
            gsem = group if group is not None else s_ca
            em.op(G, fn, writes=list(writes), dma_sem=gsem)
            castp[name] = gsem

        for cpc in range(C8):
            cast(f"w_in_{cpc}" if cpc else "w_in",
                 lambda h, cpc=cpc: h.dma_start(
                     out=w_ins_d[cpc],
                     in_=w_in_d.rearrange(
                         "(kk p) (c m) -> c p kk m",
                         p=128, c=8)[cpc]),
                 group=s_cwin)
        cast("wgx", lambda h: h.dma_start(
            out=wgx_sb[:],
            in_=w_gx_d.rearrange("i (kk p) m -> p i kk m", p=128)),
            writes=[CONSTS])
        cast("wga", lambda h: h.dma_start(
            out=wga_sb[:],
            in_=w_ga_d.rearrange("i (kk p) m -> p i kk m", p=128)),
            writes=[CONSTS])
        cast("rw", lambda h: h.dma_start(
            out=rw_sb[:], in_=rw_d.rearrange("(c p) e -> p c e", p=128)),
            writes=[CONSTS])
        cast("b_out", lambda h: h.dma_start(out=b_out_row[:],
                                            in_=b_out_d.unsqueeze(0)),
             writes=[CONSTS])
        cast("vst1", lambda h: h.dma_start(
            out=vstage[0:16, 0:128],
            in_=b_in_d.rearrange("(c p) -> c p", p=128)), writes=[buf("vst")])
        cast("vst2", lambda h: h.dma_start(
            out=vstage[0:8, 128:256],
            in_=b_gx_d.rearrange("(c p) -> c p", p=128)), writes=[buf("vst")])
        cast("vst3", lambda h: h.dma_start(
            out=vstage[0:8, 256:384],
            in_=b_ga_d.rearrange("(c p) -> c p", p=128)), writes=[buf("vst")])
        cast("vst4", lambda h: h.dma_start(
            out=vstage[0:8, 384:512],
            in_=arp_d.rearrange("(c p) -> c p", p=128)), writes=[buf("vst")])
        cast("vst5", lambda h: h.dma_start(
            out=vstage[0:8, 512:640],
            in_=aip_d.rearrange("(c p) -> c p", p=128)), writes=[buf("vst")])
        def emit_wout_cast():
            cast("w_out", lambda h: h.dma_start(out=w_outs_d, in_=w_out_d),
                 group=s_cwout)

        def emit_expert_casts(e, jlo=0, jhi=C11, w2=True):
            for j in range(jlo, jhi):
                for hh in range(2):
                    cast(f"w1_{e}_{j}" if hh == 0 else f"w1b_{e}_{j}",
                         lambda h, e=e, j=j, hh=hh: h.dma_start(
                             out=w1s_d[e, j][:, :, hh * 128:(hh + 1) * 128],
                             in_=w1_d[e].rearrange(
                                 "(kk p) (hh f) -> hh p kk f", p=128, hh=2)
                             [hh][:, :, j * 128:(j + 1) * 128]),
                         group=s_ce[e])
            if w2:
                cast(f"w2_{e}", lambda h, e=e: h.dma_start(out=w2s_d[e],
                                                           in_=w2_d[e]),
                     group=s_ce[e])

        DEFER_CASTS = True
        emit_expert_casts(0)
        emit_expert_casts(1)
        if not DEFER_CASTS:
            for _e in range(2, NE):
                emit_expert_casts(_e)
            emit_wout_cast()
        for bname in ("consts", "vst"):
            bb = buf(bname)
            bb.prod = (s_ca, s_ca.n)
            bb.prod_eng = "dma:s_ca"

        vec_specs = [(16, 0, b_in_sb), (8, 128, b_gx_sb), (8, 256, b_ga_sb),
                     (8, 384, vtmp_sb), (8, 512, ait_sb)]
        for nch, off, dst in vec_specs:
            em.op("tensor", lambda h, nch=nch, off=off: h.transpose(
                SBv(0)[:, 0:nch], vstage[0:nch, off:off + 128],
                identf[0:nch, 0:nch]),
                reads=[buf("vst"), CONSTS], writes=[banks[0]],
                wait=[(s_ca, s_ca.n)], ms=True)
            em.op("vector", lambda h, nch=nch, dst=dst: h.tensor_copy(
                dst[:, 0:nch], SBv(0)[:, 0:nch]),
                reads=[banks[0]], writes=[PCONSTS], ms=True)
        em.op("scalar", lambda h: h.activation(sp8_sb[:], vtmp_sb[:], AF.Exp),
              reads=[PCONSTS], writes=[PCONSTS])
        em.drain("scalar")
        em.op("scalar", lambda h: h.activation(sp8_sb[:], sp8_sb[:], AF.Ln,
                                               bias=cvec[:, 1:2]),
              reads=[CONSTS], writes=[PCONSTS], ms=True)
        em.op("vector", lambda h: h.tensor_scalar_mul(sp8_sb[:], sp8_sb[:],
                                                      -8.0),
              reads=[PCONSTS], writes=[PCONSTS])
        em.drain("vector")
        em.op("vector", lambda h: h.tensor_scalar_mul(ait_sb[:], ait_sb[:],
                                                      1.0 / TWO_PI),
              reads=[PCONSTS], writes=[PCONSTS], ms=True)

        pending = []
        pidx = [0]

        def fill(n=1):
            k = 0
            while pidx[0] < len(pending) and k < n:
                pending[pidx[0]]()
                pidx[0] += 1
                k += 1

        def fill_all():
            while pidx[0] < len(pending):
                pending[pidx[0]]()
                pidx[0] += 1

        def step(*fns):
            pending.append(lambda fns=fns: [f() for f in fns])

        def load_piece(ring, nm, src_fn, dst_fn):
            slots = SLOTSETS[ring]
            i = cnt[ring] % len(slots)
            cnt[ring] += 1
            slot = slots[i]
            w = []
            if nm in castp:
                gsem = castp[nm]
                w.append((gsem, gsem.n))
            em.op("sync", lambda h, i=i: h.dma_start(out=dst_fn(i),
                                                     in_=src_fn()),
                  writes=[slot], wait=w, dma_sem=slot_sems[ring][i])
            return i, slot

        def append_ln_steps(tj):
            bj = tj % 2
            Aj = accB[bj]
            ST = buf("stats")
            for sub in range(NSUB):
                step(lambda sub=sub, bj=bj: em.op(
                    "vector", lambda h, sub=sub, bj=bj: h.reduce_sum(
                        stats2[:, sub:sub + 1], acc[:, bj, sub, :], AX.X),
                    reads=[Aj], writes=[ST], ms=(sub == NSUB - 1)))
            for sub in range(NSUB):
                step(lambda sub=sub, bj=bj: (
                    em.drain("scalar") if sub > 0 else None,
                    em.op(
                        "scalar", lambda h, sub=sub, bj=bj: h.activation(
                            scr[:], acc[:, bj, sub, :], AF.Square,
                            accum_out=stats2[:, 4 + sub:5 + sub]),
                        reads=[Aj], writes=[buf("scr"), ST],
                        ms=(sub == NSUB - 1)))[-1])
            step(lambda: em.op("vector", lambda h: h.tensor_scalar_mul(
                stats2[:, 8:12], stats2[:, 0:4], 1.0 / D),
                reads=[ST], writes=[ST]))
            step(lambda: em.op("vector", lambda h: h.tensor_scalar_mul(
                stats2[:, 12:16], stats2[:, 4:8], 1.0 / D), writes=[ST]))
            step(lambda: em.drain("vector"),
                 lambda: em.op("vector", lambda h: h.tensor_tensor(
                     stats2[:, 16:20], stats2[:, 8:12], stats2[:, 8:12],
                     ALU.mult), writes=[ST]))
            step(lambda: em.drain("vector"),
                 lambda: em.op("vector", lambda h: h.tensor_tensor(
                     stats2[:, 20:24], stats2[:, 12:16], stats2[:, 16:20],
                     ALU.subtract), writes=[ST]))
            step(lambda: em.drain("vector"),
                 lambda: em.op("vector", lambda h: h.tensor_scalar_add(
                     stats2[:, 24:28], stats2[:, 20:24], 1e-5),
                     writes=[ST], ms=True))
            step(lambda: em.op("scalar", lambda h: h.activation(
                stats2[:, 28:32], stats2[:, 24:28], AF.Ln),
                reads=[ST], writes=[ST]))
            step(lambda: em.drain("scalar"),
                 lambda: em.op("scalar", lambda h: h.activation(
                     stats2[:, 28:32], stats2[:, 28:32], AF.Exp,
                     scale=cvec[:, 0:1]),
                     reads=[CONSTS], writes=[ST], ms=True))
            for sub in range(NSUB):
                step(lambda sub=sub, bj=bj: em.op(
                    "vector", lambda h, sub=sub, bj=bj: h.tensor_scalar(
                        xn_tok[:, sub, :], acc[:, bj, sub, :],
                        stats2[:, 8 + sub:9 + sub],
                        stats2[:, 28 + sub:29 + sub],
                        ALU.subtract, ALU.mult),
                    reads=[Aj, ST], writes=[buf("xn_tok")],
                    ms=(sub == NSUB - 1)))

        def append_scan_steps(c, sq):
            CARR = buf("carr")
            GA = buf(f"ga{c}")
            GX = buf(f"gx{c}")
            PE = "vector" if c >= 6 else "gpsimd"
            step(lambda c=c, GA=GA: em.op(
                "scalar", lambda h, c=c: h.activation(
                    bM[:], ga_all[:, c, :], AF.Exp,
                    scale=sp8_sb[:, c:c + 1]),
                reads=[GA, PCONSTS], writes=[buf("bM")], ms=True))
            step(lambda c=c, GA=GA: em.op(
                "vector", lambda h, c=c: h.tensor_scalar_mul(
                    bU2[:], ga_all[:, c, :], ait_sb[:, c:c + 1]),
                reads=[GA, PCONSTS], writes=[buf("bU2")], ms=True))
            step(lambda: em.op("vector", lambda h: h.tensor_tensor_scan(
                bL[:], bU2[:], bU2[:], 0.0, ALU.add, ALU.bypass),
                reads=[buf("bU2")], writes=[buf("bL")]),
                lambda: em.drain("vector"))
            step(lambda: em.op("vector", lambda h: h.tensor_copy(
                bI[:], bL[:]), writes=[buf("bIG")]))
            step(lambda: em.op("vector", lambda h: h.tensor_copy(
                bK[:], bI[:]), reads=[buf("bIG")], writes=[buf("bKG")]))
            step(lambda: em.op("vector", lambda h: h.tensor_tensor(
                bF[:], bL[:], bK[:], ALU.subtract),
                reads=[buf("bKG"), buf("bL")], writes=[buf("bF")], ms=True))
            step(lambda: em.op("scalar", lambda h: h.activation(
                bS[:], bF[:], AF.Sin, scale=cvec[:, 2:3]),
                reads=[buf("bF"), CONSTS], writes=[buf("bS")], ms=True))
            step(lambda: em.op("vector", lambda h: h.tensor_scalar_add(
                bU2[:], bL[:], 0.25),
                reads=[buf("bL")], writes=[buf("bU2")]))
            step(lambda: em.op("vector", lambda h: h.tensor_copy(
                bI[:], bU2[:]), writes=[buf("bIG")]))
            step(lambda: em.op("vector", lambda h: h.tensor_copy(
                bK[:], bI[:]), reads=[buf("bIG")], writes=[buf("bKG")]))
            step(lambda: em.op("vector", lambda h: h.tensor_tensor(
                bF[:], bU2[:], bK[:], ALU.subtract),
                reads=[buf("bKG"), buf("bU2")], writes=[buf("bF")], ms=True))
            step(lambda: em.op("scalar", lambda h: h.activation(
                bC[:], bF[:], AF.Sin, scale=cvec[:, 2:3]),
                reads=[buf("bF"), CONSTS], writes=[buf("bC")], ms=True))
            step(lambda: em.op("vector", lambda h: h.tensor_tensor(
                bU2[:], bM[:], bM[:], ALU.mult),
                reads=[buf("bM")], writes=[buf("bU2")]))
            step(lambda: em.op("vector", lambda h: h.tensor_scalar(
                bU1[:], bU2[:], -1.0, 1.0, ALU.mult, ALU.add),
                reads=[buf("bU2")], writes=[buf("bU1")]))
            step(lambda: em.op("vector", lambda h: h.tensor_scalar_max(
                bU2[:], bU1[:], 1e-6),
                reads=[buf("bU1")], writes=[buf("bU2")], ms=True))
            step(lambda: em.op("scalar", lambda h: h.activation(
                bF[:], bU2[:], AF.Sqrt),
                reads=[buf("bU2")], writes=[buf("bF")], ms=True))
            step(lambda c=c, GX=GX, PE=PE: em.op(
                PE, lambda h, c=c: h.tensor_tensor(
                    bU1[:], gx_all[:, c, :], bF[:], ALU.mult),
                reads=[GX, buf("bF")], writes=[buf("bU1")]))
            step(lambda c=c, PE=PE: em.op(PE, lambda h, c=c: h.tensor_tensor(
                bWre[:], bU1[:], xhbf[:, c, :], ALU.mult),
                reads=[buf("bU1"), buf(f"xh{c}")], writes=[buf("bWre")]))
            step(lambda c=c, PE=PE: em.op(PE, lambda h, c=c: h.tensor_tensor(
                bWim[:], bU1[:], xhbf[:, c + 8, :], ALU.mult),
                reads=[buf(f"xh{c + 8}")], writes=[buf("bWim")], ms=True))
            step(lambda PE=PE: em.op(PE, lambda h: h.tensor_tensor(
                bU1[:], bWre[:], bC[:], ALU.mult),
                reads=[buf("bWre"), buf("bC")], writes=[buf("bU1")]),
                lambda PE=PE: em.op(PE, lambda h: h.tensor_tensor(
                    bU2[:], bWim[:], bS[:], ALU.mult),
                    reads=[buf("bWim"), buf("bS")], writes=[buf("bU2")]))
            step(lambda PE=PE: em.op(PE, lambda h: h.tensor_tensor(
                bGre[:], bU1[:], bU2[:], ALU.add),
                writes=[buf("bKG")]))
            step(lambda PE=PE: em.op(PE, lambda h: h.tensor_tensor(
                bU1[:], bWim[:], bC[:], ALU.mult), writes=[buf("bU1")]),
                lambda PE=PE: em.op(PE, lambda h: h.tensor_tensor(
                    bU2[:], bWre[:], bS[:], ALU.mult), writes=[buf("bU2")]))
            step(lambda PE=PE: em.op(PE, lambda h: h.tensor_tensor(
                bGim[:], bU1[:], bU2[:], ALU.subtract),
                writes=[buf("bIG")], ms=True))
            step(lambda c=c, sq=sq: em.op(
                "vector", lambda h, c=c, sq=sq: h.tensor_tensor_scan(
                    bWre[:], bM[:], bGre[:], carr_re[:, c, sq:sq + 1],
                    ALU.mult, ALU.add),
                reads=[buf("bM"), buf("bKG"), CARR], writes=[buf("bWre")]))
            step(lambda c=c, sq=sq: em.op(
                "vector", lambda h, c=c, sq=sq: h.tensor_tensor_scan(
                    bWim[:], bM[:], bGim[:], carr_im[:, c, sq:sq + 1],
                    ALU.mult, ALU.add),
                reads=[buf("bM"), buf("bIG"), CARR], writes=[buf("bWim")]),
                lambda: em.barrier("vector"))
            step(lambda PE=PE: em.op(PE, lambda h: h.tensor_tensor(
                bU1[:], bC[:], bWre[:], ALU.mult),
                reads=[buf("bC"), buf("bWre")], writes=[buf("bU1")]),
                lambda PE=PE: em.op(PE, lambda h: h.tensor_tensor(
                    bU2[:], bS[:], bWim[:], ALU.mult),
                    reads=[buf("bS"), buf("bWim")], writes=[buf("bU2")]))
            step(lambda c=c, PE=PE: em.op(PE, lambda h, c=c: h.tensor_tensor(
                hm[:, c, :], bU1[:], bU2[:], ALU.subtract),
                writes=[buf(f"hm{c}")]))
            step(lambda PE=PE: em.op(PE, lambda h: h.tensor_tensor(
                bU1[:], bC[:], bWim[:], ALU.mult), writes=[buf("bU1")]),
                lambda PE=PE: em.op(PE, lambda h: h.tensor_tensor(
                    bU2[:], bS[:], bWre[:], ALU.mult), writes=[buf("bU2")]))
            step(lambda c=c, PE=PE: em.op(PE, lambda h, c=c: h.tensor_tensor(
                hm[:, c + 8, :], bU1[:], bU2[:], ALU.add),
                writes=[buf(f"hm{c}")], ms=True))
            e511 = slice(T - 1, T)
            step(lambda PE=PE: em.op(PE, lambda h: h.tensor_tensor(
                ctmp[:, 0:1], bC[:, e511], bWre[:, e511], ALU.mult),
                writes=[buf("ctmp")]),
                lambda PE=PE: em.op(PE, lambda h: h.tensor_tensor(
                    ctmp[:, 1:2], bS[:, e511], bWim[:, e511], ALU.mult),
                    writes=[buf("ctmp")]),
                lambda PE=PE: em.op(PE, lambda h: h.tensor_tensor(
                    ctmp[:, 2:3], bC[:, e511], bWim[:, e511], ALU.mult),
                    writes=[buf("ctmp")]),
                lambda PE=PE: em.op(PE, lambda h: h.tensor_tensor(
                    ctmp[:, 3:4], bS[:, e511], bWre[:, e511], ALU.mult),
                    writes=[buf("ctmp")]),
                lambda PE=PE: em.drain(PE))
            step(lambda c=c, sq=sq, PE=PE: em.op(
                PE, lambda h, c=c, sq=sq: h.tensor_tensor(
                    carr_re[:, c, sq:sq + 1], ctmp[:, 0:1], ctmp[:, 1:2],
                    ALU.subtract), reads=[buf("ctmp")], writes=[CARR]),
                lambda c=c, sq=sq, PE=PE: em.op(
                    PE, lambda h, c=c, sq=sq: h.tensor_tensor(
                        carr_im[:, c, sq:sq + 1], ctmp[:, 2:3], ctmp[:, 3:4],
                        ALU.add), reads=[buf("ctmp")], writes=[CARR],
                    ms=True))

        bdl_parity = [0]

        def emit_bdl(c):
            bsel = bdl_parity[0] % 2
            bdl_parity[0] += 1
            bA, bB = (4, 5) if bsel == 0 else (6, 7)
            for kk in range(2):
                m = 2 * c + kk
                em.op("tensor", lambda h, c=c, kk=kk, m=m, bA=bA: h.matmul(
                    SBv(bA), wgx_sb[:, c, kk, :], xhbf[:, m, :],
                    start=(kk == 0), stop=(kk == 1)),
                    reads=[buf(f"xh{m}"), CONSTS], writes=[banks[bA]],
                    ms=(kk == 1))
            for kk in range(2):
                m = 2 * c + kk
                em.op("tensor", lambda h, c=c, kk=kk, m=m, bB=bB: h.matmul(
                    SBv(bB), wga_sb[:, c, kk, :], xhbf[:, m, :],
                    start=(kk == 0), stop=(kk == 1)),
                    reads=[buf(f"xh{m}"), CONSTS], writes=[banks[bB]],
                    ms=(kk == 1))
            em.op("scalar", lambda h, c=c, bA=bA: h.activation(
                gx_all[:, c, :], SBv(bA), AF.Sigmoid,
                bias=b_gx_sb[:, c:c + 1]),
                reads=[banks[bA], PCONSTS], writes=[buf(f"gx{c}")], ms=True)
            em.op("scalar", lambda h, c=c, bB=bB: h.activation(
                ga_all[:, c, :], SBv(bB), AF.Sigmoid,
                bias=b_ga_sb[:, c:c + 1]),
                reads=[banks[bB], PCONSTS], writes=[buf(f"ga{c}")], ms=True)

        def emit_ffn_expert(e, bi):
            A = accB[bi]
            for j in range(C11):
                si, slot = load_piece(
                    "w1", f"w1_{e}_{j}",
                    lambda e=e, j=j: w1s_d[e, j],
                    lambda i: w1_r[:, i])
                i1 = 2 * (j % 4)
                i2 = i1 + 1
                for kk in range(C8):
                    em.op("tensor", lambda h, si=si, kk=kk, i1=i1: h.matmul(
                        SBv(i1), w1_r[:, si, kk, 0:128], xnf[:, kk, :],
                        start=(kk == 0), stop=(kk == C8 - 1)),
                        reads=[slot, buf("xnf")], writes=[banks[i1]],
                        ms=(kk == C8 - 1))
                for kk in range(C8):
                    em.op("tensor", lambda h, si=si, kk=kk, i2=i2: h.matmul(
                        SBv(i2), w1_r[:, si, kk, 128:256], xnf[:, kk, :],
                        start=(kk == 0), stop=(kk == C8 - 1)),
                        reads=[slot, buf("xnf")], writes=[banks[i2]],
                        ms=(kk == C8 - 1))
                sB = s1tB[j % 2]
                em.op("scalar", lambda h, j=j, i1=i1: h.activation(
                    s1t[:, j % 2, :], SBv(i1), AF.Silu),
                    reads=[banks[i1]], writes=[sB], ms=True)
                em.op("vector", lambda h, j=j, i2=i2: h.tensor_tensor(
                    g_sb[:, j, :], s1t[:, j % 2, :], SBv(i2), ALU.mult),
                    reads=[sB, banks[i2]], writes=[buf("g_sb")], ms=True)
                fill(1)
            for kk in range(C11):
                si, slot = load_piece(
                    "w2", f"w2_{e}",
                    lambda e=e, kk=kk: w2s_d[e, kk * 128:(kk + 1) * 128, :],
                    lambda i: w2_r[:, i])
                for tsub in range(NSUB):
                    for hh in range(2):
                        em.op("tensor",
                              lambda h, si=si, kk=kk, tsub=tsub, hh=hh:
                              h.matmul(
                                  pBig[tsub][:, hh * T:(hh + 1) * T],
                                  g_sb[:, kk, tsub * 128:(tsub + 1) * 128],
                                  w2_r[:, si, hh * T:(hh + 1) * T],
                                  start=(kk == 0), stop=(kk == C11 - 1)),
                              reads=[slot, buf("g_sb")],
                              writes=[banks[2 * tsub + hh]],
                              ms=(kk == C11 - 1 or
                                  (tsub == NSUB - 1 and hh == 1)))
                fill(2)
            for bank in range(8):
                tsub, hh = bank // 2, bank % 2
                em.op("vector", lambda h, tsub=tsub, hh=hh, e=e, bi=bi:
                      h.scalar_tensor_tensor(
                          acc[:, bi, tsub, hh * T:(hh + 1) * T],
                          pBig[tsub][:, hh * T:(hh + 1) * T],
                          lp_sb[:, tsub * 8 + e:tsub * 8 + e + 1],
                          acc[:, bi, tsub, hh * T:(hh + 1) * T],
                          ALU.mult, ALU.add),
                      reads=[banks[bank], buf("lp"), A],
                      writes=[A], ms=True)
            fill(4)

        def emit_attn_out(bi):
            A = accB[bi]
            kks = [c + 8 * q for c in range(C8) for q in range(2)]
            for ki, kk in enumerate(kks):
                si, slot = load_piece(
                    "wout", "w_out",
                    lambda kk=kk: w_outs_d.rearrange(
                        "(kk p) m -> kk p m", p=128)[kk],
                    lambda i: wout_r[:, i])
                for tsub in range(NSUB):
                    for hh in range(2):
                        em.op("tensor",
                              lambda h, si=si, kk=kk, tsub=tsub, hh=hh,
                              ki=ki:
                              h.matmul(
                                  pBig[tsub][:, hh * T:(hh + 1) * T],
                                  hm[:, kk, tsub * 128:(tsub + 1) * 128],
                                  wout_r[:, si, hh * T:(hh + 1) * T],
                                  start=(ki == 0), stop=False),
                              reads=[slot, buf(f"hm{kk % 8}")],
                              writes=[banks[2 * tsub + hh]],
                              ms=(tsub == NSUB - 1 and hh == 1))
            for tsub in range(NSUB):
                for hh in range(2):
                    em.op("tensor", lambda h, tsub=tsub, hh=hh: h.matmul(
                        pBig[tsub][:, hh * T:(hh + 1) * T],
                        onesb[0:1, 0:128],
                        b_out_row[0:1, hh * T:(hh + 1) * T],
                        start=False, stop=True),
                        reads=[CONSTS],
                        writes=[banks[2 * tsub + hh]],
                        ms=(hh == 1))
            if debug and bi == 0 and not dbg_done:
                dbg_done.append(1)
                em.op("vector", lambda h: h.tensor_copy(bL[:], pBig[0][:, 0:T]),
                      reads=[banks[0]], writes=[buf("bL")], ms=True)
                em.op(G, lambda h: h.dma_start(out=dbg_ps, in_=bL[:]),
                      reads=[buf("bL")], dma_sem=s_st[0])
            for tsub in (3, 0, 1, 2):
                for hh in range(2):
                    em.op("vector", lambda h, tsub=tsub, hh=hh, bi=bi:
                          h.tensor_tensor(
                              acc[:, bi, tsub, hh * T:(hh + 1) * T],
                              acc[:, bi, tsub, hh * T:(hh + 1) * T],
                              pBig[tsub][:, hh * T:(hh + 1) * T], ALU.add),
                          reads=[banks[2 * tsub + hh], A],
                          writes=[A], ms=True)
            if debug and bi == 0 and len(dbg_done) == 1:
                dbg_done.append(2)
                em.op(G, lambda h: h.dma_start(out=dbg_ac2, in_=acc[:, 0]),
                      reads=[A], dma_sem=s_st[0])
                em.op(G, lambda h: h.dma_start(out=dbg_hm2, in_=hm[:]),
                      reads=[buf("hm")], dma_sem=s_st[0])

        def emit_router():
            rb = banks[4]
            for tsub in range(NSUB):
                for c in range(C8):
                    em.op("tensor", lambda h, tsub=tsub, c=c: h.matmul(
                        SBv(4)[:, tsub * 8:(tsub + 1) * 8],
                        xnf[:, c, tsub * 128:(tsub + 1) * 128],
                        rw_sb[:, c, :], start=(c == 0), stop=(c == C8 - 1)),
                        reads=[buf("xnf"), CONSTS], writes=[rb],
                        ms=(c == C8 - 1))
            em.op("vector", lambda h: h.tensor_copy(lg[:], SBv(4)[:, 0:32]),
                  reads=[rb], writes=[buf("lg")], ms=True)

            def r3(ap):
                return ap.rearrange("p (gr e) -> p gr e", gr=4)

            def b4(ap):
                return ap.unsqueeze(2).broadcast_to([128, 4, 8])

            V = "vector"
            em.drain(V)
            em.op(V, lambda h: h.tensor_reduce(
                rm4[:], r3(lg[:]), AX.X, ALU.max), reads=[buf("lg")],
                writes=[buf("rm4")])
            em.drain(V)
            em.op(V, lambda h: h.tensor_tensor(
                r3(rex[:]), r3(lg[:]), b4(rm4[:]), ALU.subtract),
                reads=[buf("lg"), buf("rm4")], writes=[buf("rex")], ms=True)
            em.op("scalar", lambda h: h.activation(rex[:], rex[:], AF.Exp),
                  reads=[buf("rex")], writes=[buf("rex")], ms=True)
            em.drain(V)
            em.op(V, lambda h: h.tensor_reduce(
                rs4[:], r3(rex[:]), AX.X, ALU.add), reads=[buf("rex")],
                writes=[buf("rs4")])
            em.drain(V)
            em.op(V, lambda h: h.reciprocal(rs4[:], rs4[:]),
                  writes=[buf("rs4")])
            em.drain(V)
            em.op(V, lambda h: h.tensor_tensor(
                r3(rpr[:]), r3(rex[:]), b4(rs4[:]), ALU.mult),
                reads=[buf("rex"), buf("rs4")], writes=[buf("rpr")])
            em.drain(V)
            em.op(V, lambda h: h.tensor_reduce(
                rv4[:], r3(rpr[:]), AX.X, ALU.max), reads=[buf("rpr")],
                writes=[buf("rv4")])
            em.drain(V)
            em.op(V, lambda h: h.tensor_tensor(
                r3(rt1[:]), r3(rpr[:]), b4(rv4[:]), ALU.is_equal),
                reads=[buf("rpr"), buf("rv4")], writes=[buf("rt1")])
            em.drain(V)
            em.op(V, lambda h: h.tensor_tensor(
                rt2[:], rt1[:], idxmB[:], ALU.mult),
                reads=[buf("rt1"), CONSTS], writes=[buf("rt2")])
            em.drain(V)
            em.op(V, lambda h: h.tensor_scalar_add(
                rt3[:], rt2[:], 1024.0), reads=[buf("rt2")],
                writes=[buf("rt3")])
            em.drain(V)
            em.op(V, lambda h: h.tensor_reduce(
                rv4[:], r3(rt3[:]), AX.X, ALU.min), reads=[buf("rt3")],
                writes=[buf("rv4")])
            em.drain(V)
            em.op(V, lambda h: h.tensor_tensor(
                r3(rt1[:]), r3(idxc[:]), b4(rv4[:]), ALU.is_equal),
                reads=[CONSTS, buf("rv4")], writes=[buf("rt1")])
            em.drain(V)
            em.op(V, lambda h: h.scalar_tensor_tensor(
                rt2[:], rt1[:], -1e9, rpr[:], ALU.mult, ALU.add),
                reads=[buf("rt1"), buf("rpr")], writes=[buf("rt2")])
            em.drain(V)
            em.op(V, lambda h: h.tensor_reduce(
                rv4[:], r3(rt2[:]), AX.X, ALU.max), reads=[buf("rt2")],
                writes=[buf("rv4")])
            em.drain(V)
            em.op(V, lambda h: h.tensor_tensor(
                r3(rt3[:]), r3(rt2[:]), b4(rv4[:]), ALU.is_equal),
                reads=[buf("rt2"), buf("rv4")], writes=[buf("rt3")])
            em.drain(V)
            em.op(V, lambda h: h.tensor_tensor(
                rt2[:], rt3[:], idxmB[:], ALU.mult),
                reads=[buf("rt3"), CONSTS], writes=[buf("rt2")])
            em.drain(V)
            em.op(V, lambda h: h.tensor_scalar_add(
                rt3[:], rt2[:], 1024.0), reads=[buf("rt2")],
                writes=[buf("rt3")])
            em.drain(V)
            em.op(V, lambda h: h.tensor_reduce(
                rv4[:], r3(rt3[:]), AX.X, ALU.min), reads=[buf("rt3")],
                writes=[buf("rv4")])
            em.drain(V)
            em.op(V, lambda h: h.tensor_tensor(
                r3(rt2[:]), r3(idxc[:]), b4(rv4[:]), ALU.is_equal),
                reads=[CONSTS, buf("rv4")], writes=[buf("rt2")])
            em.drain(V)
            em.op(V, lambda h: h.tensor_tensor(
                rt3[:], rt1[:], rt2[:], ALU.add),
                reads=[buf("rt1"), buf("rt2")], writes=[buf("rt3")])
            em.drain(V)
            em.op(V, lambda h: h.tensor_tensor(
                lp_sb[:], rpr[:], rt3[:], ALU.mult),
                reads=[buf("rpr"), buf("rt3")], writes=[buf("lp")], ms=True)

        dbg_done = []
        append_ln_steps(0)
        fill_all()

        for ti in range(n_tiles_total):
            t, sq = divmod(ti, n_seq)
            bi = ti % 2
            A = accB[bi]
            del pending[:]
            pidx[0] = 0

            for c in range(C8):
                td = TDv[c % 2]
                bk = banks[6 + c % 2]
                for sub in range(NSUB):
                    em.op("tensor", lambda h, c=c, sub=sub, td=td: h.transpose(
                        td[:, sub * 128:(sub + 1) * 128],
                        xn_tok[:, sub, c * 128:(c + 1) * 128], ident[:]),
                        reads=[buf("xn_tok"), CONSTS], writes=[bk],
                        ms=(sub == NSUB - 1))
                em.op("vector", lambda h, c=c, td=td: h.tensor_copy(
                    xnf[:, c, :], td[:, 0:T]),
                    reads=[bk], writes=[buf("xnf")], ms=True)

            emit_router()

            hidx = 0
            for piece in range(C8):
                si, slot = load_piece(
                    "win", "w_in",
                    lambda piece=piece: w_ins_d[piece],
                    lambda i: win_r[:, i])
                for half in range(2):
                    m = 2 * piece + half
                    bk_i = hidx % 4
                    hidx += 1
                    for kk in range(C8):
                        em.op("tensor",
                              lambda h, si=si, kk=kk, half=half, bk_i=bk_i:
                              h.matmul(SBv(bk_i),
                                       win_r[:, si, kk,
                                             half * 128:(half + 1) * 128],
                                       xnf[:, kk, :], start=(kk == 0),
                                       stop=(kk == C8 - 1)),
                              reads=[slot, buf("xnf")], writes=[banks[bk_i]],
                              ms=(kk == C8 - 1))
                    em.op("vector", lambda h, m=m, bk_i=bk_i: h.tensor_scalar_add(
                        xhbf[:, m, :], SBv(bk_i), b_in_sb[:, m:m + 1]),
                        reads=[banks[bk_i], PCONSTS], writes=[buf(f"xh{m}")],
                        ms=True)
                    fill(1)
                emit_bdl(piece)
                if piece >= 4:
                    for c in (2 * (piece - 4), 2 * (piece - 4) + 1):
                        append_scan_steps(c, sq)
                fill(1)

            for e in range(NE):
                if ti == 0 and DEFER_CASTS:
                    if e < 6:
                        emit_expert_casts(e + 2, 0, 6, w2=False)
                    if 1 <= e < 7:
                        emit_expert_casts(e + 1, 6, C11, w2=True)
                    if e == 4:
                        emit_wout_cast()
                if e == 2 and ti + 1 < n_tiles_total:
                    append_ln_steps(ti + 1)
                emit_ffn_expert(e, bi)

            fill_all()
            if debug and ti == 0:
                em.op(G, lambda h: h.dma_start(out=dbg_xh, in_=xhbf[:]),
                      reads=[buf(f"xh{m}") for m in range(C16)], dma_sem=s_st[0])
                em.op(G, lambda h: h.dma_start(out=dbg_gx, in_=gx_all[:]),
                      reads=[buf(f"gx{c}") for c in range(C8)], dma_sem=s_st[0])
                em.op(G, lambda h: h.dma_start(out=dbg_ga, in_=ga_all[:]),
                      reads=[buf(f"ga{c}") for c in range(C8)], dma_sem=s_st[0])
                em.op(G, lambda h: h.dma_start(out=dbg_hm, in_=hm[:]),
                      reads=[buf("hm")] + [buf(f"hm{c}") for c in range(C8)], dma_sem=s_st[0])
                em.op(G, lambda h: h.dma_start(out=dbg_xn, in_=xnf[:]),
                      reads=[buf("xnf")], dma_sem=s_st[0])
                em.op(G, lambda h, bi=bi: h.dma_start(out=dbg_ac, in_=acc[:, bi]),
                      reads=[accB[bi]], dma_sem=s_st[0])
                em.op(G, lambda h: h.dma_start(out=dbg_lp, in_=lp_sb[:]),
                      reads=[buf("lp")], dma_sem=s_st[0])
                em.op(G, lambda h: h.dma_start(out=dbg_wo, in_=w_outs_d),
                      wait=[(s_cwout, s_cwout.n)], dma_sem=s_st[0])
                em.op(G, lambda h: h.dma_start(out=dbg_wi, in_=w_ins_d),
                      wait=[(s_cwin, s_cwin.n)], dma_sem=s_st[0])
            emit_attn_out(bi)

            em.op(G, lambda h, t=t, sq=sq, bi=bi: h.dma_start(
                out=out_d[sq, t * T:(t + 1) * T, :].rearrange(
                    "(sub p) d -> p sub d", p=128),
                in_=acc[:, bi]), reads=[A], dma_sem=s_st[bi])
            if ti + 2 < n_tiles_total:
                emit_xload(ti + 2)

        em.raw(G, lambda h: h.nop(),
               wait=[(s_st[0], s_st[0].n), (s_st[1], s_st[1].n)])

        counts = {k.name: k.n for k in (s_t, s_v, s_a, s_p)}
        n_instr = sum(len(v) for v in em.cmds.values())
        print(f"[build] instrs={n_instr} sems={counts}")

        blk = stack.enter_context(nc.Block())

        @blk.sync
        def _(h):
            for cmd in em.cmds["sync"]:
                cmd(h)

        @blk.gpsimd
        def _(h):
            for cmd in em.cmds["gpsimd"]:
                cmd(h)

        @blk.scalar
        def _(h):
            for cmd in em.cmds["scalar"]:
                cmd(h)

        @blk.vector
        def _(h):
            for cmd in em.cmds["vector"]:
                cmd(h)

        @blk.tensor
        def _(h):
            for cmd in em.cmds["tensor"]:
                cmd(h)

    return nc


_NC_CACHE = {}


def _get_program(n_seq, n_tile):
    key = (n_seq, n_tile)
    if key not in _NC_CACHE:
        _NC_CACHE[key] = build_program(n_seq, n_tile)
    return _NC_CACHE[key]


def make_in_maps(x, weights, n_cores):
    per = x.shape[0] // n_cores
    in_maps = []
    for c in range(n_cores):
        m = {"x": np.ascontiguousarray(x[per * c:per * (c + 1)])}
        m.update(weights)
        in_maps.append(m)
    return in_maps


def kernel(L0_attn_x, L0_attn_w_in, L0_attn_b_in, L0_attn_w_gx, L0_attn_b_gx,
           L0_attn_w_ga, L0_attn_b_ga, L0_attn_a_real_param,
           L0_attn_a_imag_param, L0_attn_w_out, L0_attn_b_out,
           L0_ffn_router_weight, L0_ffn_w1, L0_ffn_w2):
    x = np.ascontiguousarray(np.asarray(L0_attn_x, dtype=np.float32))
    weights = dict(
        w_in=np.ascontiguousarray(np.asarray(L0_attn_w_in, np.float32)),
        b_in=np.ascontiguousarray(np.asarray(L0_attn_b_in, np.float32)),
        w_gx=np.ascontiguousarray(np.asarray(L0_attn_w_gx, np.float32)),
        b_gx=np.ascontiguousarray(np.asarray(L0_attn_b_gx, np.float32)),
        w_ga=np.ascontiguousarray(np.asarray(L0_attn_w_ga, np.float32)),
        b_ga=np.ascontiguousarray(np.asarray(L0_attn_b_ga, np.float32)),
        arp=np.ascontiguousarray(np.asarray(L0_attn_a_real_param, np.float32)),
        aip=np.ascontiguousarray(np.asarray(L0_attn_a_imag_param, np.float32)),
        w_out=np.ascontiguousarray(np.asarray(L0_attn_w_out, np.float32)),
        b_out=np.ascontiguousarray(np.asarray(L0_attn_b_out, np.float32)),
        rw=np.ascontiguousarray(np.asarray(L0_ffn_router_weight, np.float32)),
        w1=np.ascontiguousarray(np.asarray(L0_ffn_w1, np.float32)),
        w2=np.ascontiguousarray(np.asarray(L0_ffn_w2, np.float32)),
    )
    nc = _get_program(2, 8)
    in_maps = make_in_maps(x, weights, 8)
    res = run_bass_kernel_spmd(nc, in_maps, core_ids=list(range(8)))
    return np.concatenate([r["out"] for r in res.results], axis=0)
